# version 02
# baseline (speedup 1.0000x reference)
from collections import defaultdict

import numpy as np
import concourse.bass as bass
import concourse.mybir as mybir
from concourse.bass_utils import run_bass_kernel_spmd

F32 = mybir.dt.float32
BF16 = mybir.dt.bfloat16
AF = mybir.ActivationFunctionType
ALU = mybir.AluOpType
AX = mybir.AxisListType

T, D, H, DH = 8192, 2048, 16, 128
NT = T // 128
TPC = 8
OWN = TPC * 128
DFF = 8192
MAGIC = 12582912.0
INV_TAU = 1.0 / float(np.float32(np.sqrt(np.float32(T))))
EPS_LN = 1e-5
EPS_Q = 1e-5
NW_EL = float(DFF * D)


def tiles_of_core(c):
    return [8 * k + (c + k) % 8 for k in range(TPC)]


def build_program():
    nc = bass.Bass("TRN2", target_bir_lowering=False,
                   dynamic_dma_scratch_size=16384)

    x_full = nc.dram_tensor("x_full", [T, D], F32, kind="ExternalInput").ap()
    x_own = nc.dram_tensor("x_own", [OWN, D], F32, kind="ExternalInput").ap()
    msk = nc.dram_tensor("msk", [128, 288], F32, kind="ExternalInput").ap()
    Wq = nc.dram_tensor("Wq", [D, D], F32, kind="ExternalInput").ap()
    Wk = nc.dram_tensor("Wk", [D, D], F32, kind="ExternalInput").ap()
    Wv = nc.dram_tensor("Wv", [D, D], F32, kind="ExternalInput").ap()
    Wo = nc.dram_tensor("Wo", [D, D], F32, kind="ExternalInput").ap()
    wg = nc.dram_tensor("wg", [DFF, D], F32, kind="ExternalInput").ap()
    wu = nc.dram_tensor("wu", [DFF, D], F32, kind="ExternalInput").ap()
    wd = nc.dram_tensor("wd", [D, DFF], F32, kind="ExternalInput").ap()
    Y = nc.dram_tensor("Y", [OWN, D], F32, kind="ExternalOutput").ap()

    xnf_d = nc.dram_tensor("xnf_d", [NT, 16, 128, 128], BF16).ap()
    q_d = nc.dram_tensor("q_d", [16, 128, OWN], BF16).ap()
    xq_d = nc.dram_tensor("xq_d", [16, 128, OWN], BF16).ap()
    ao_d = nc.dram_tensor("ao_d", [OWN, D], BF16).ap()
    res_d = nc.dram_tensor("res_d", [OWN, D], F32).ap()
    h_d = nc.dram_tensor("h_d", [OWN, DFF], BF16).ap()
    wgq_d = nc.dram_tensor("wgq_d", [DFF, D], BF16).ap()
    wuq_d = nc.dram_tensor("wuq_d", [DFF, D], BF16).ap()
    wdq_d = nc.dram_tensor("wdq_d", [D, DFF], BF16).ap()
    wdqT_d = nc.dram_tensor("wdqT_d", [8, 128, 64, 256], BF16).ap()

    OPS = []
    CNT = defaultdict(int)

    def op(eng, fn, waits=(), inc=None):
        ws = [(s, v) for (s, v) in waits if v > 0]
        if inc is not None:
            CNT[inc[0]] += inc[1]
        OPS.append((eng, tuple(ws), fn, inc))
        return CNT[inc[0]] if inc is not None else None

    def now(sem):
        return CNT[sem]

    import contextlib
    ctx = contextlib.ExitStack()
    with ctx:
        S = {n: ctx.enter_context(nc.semaphore(n))
             for n in ["ldx", "ldg", "lds", "stx", "ve", "ac", "pe", "po"]}

        def sb(name, p, f, dt=BF16):
            return ctx.enter_context(nc.sbuf_tensor(name, [p, f], dt))

        BIGA = sb("BIGA", 128, 32768)
        BIGB = sb("BIGB", 128, 33024)
        BIGC = sb("BIGC", 128, 16384)
        BIGD = sb("BIGD", 128, 16384)
        FA = sb("FA", 128, 3072, F32)
        ident = sb("ident", 128, 128, BF16)
        msk_sb = sb("msk_sb", 128, 288, F32)
        ones_col = sb("ones_col", 128, 1, BF16)
        ones_row = sb("ones_row", 1, 128, BF16)
        onesf_col = sb("onesf_col", 128, 1, F32)
        onesf_row = sb("onesf_row", 1, 128, F32)
        sx_sb = sb("sx_sb", 128, TPC, F32)
        accw = sb("accw", 128, 3, F32)
        tot_sb = sb("tot_sb", 1, 3, F32)
        invs_sb = sb("invs_sb", 1, 3, F32)
        invs_b = sb("invs_b", 128, 3, F32)
        sw_b = sb("sw_b", 128, 3, F32)
        hmax = sb("hmax", 128, TPC, F32)
        dsc_sb = sb("dsc_sb", 128, TPC, F32)
        sxg_sb = sb("sxg_sb", 128, TPC, F32)
        su_sb = sb("su_sb", 128, TPC, F32)
        qsc_sb = sb("qsc_sb", 128, TPC, F32)
        mc_sb = sb("mc_sb", 128, TPC, F32)
        bn6 = sb("bn6", 128, 24, F32)
        mv2 = sb("mv2", 128, 2, F32)
        vv = [sb(f"vv{i}", 128, 1, F32) for i in range(2)]
        sd = [sb(f"sd{i}", 128, 1, F32) for i in range(2)]
        rstd = [sb(f"rstd{i}", 128, 1, F32) for i in range(2)]
        nmr = [sb(f"nmr{i}", 128, 1, F32) for i in range(2)]
        am1 = sb("am1", 128, 1, F32)
        zv = sb("zv", 128, 1, F32)
        zr = sb("zr", 128, 1, F32)

        XO = [FA.ap()[:, 0:2048],
              BIGB.ap()[:, 4096:8192].bitcast(F32)]
        XN32 = BIGB.ap()[:, 0:4096].bitcast(F32)
        T1S = BIGA.ap()[:, 26624:30720].bitcast(F32)
        WCH = [FA.ap()[:, 0:1024], FA.ap()[:, 1024:2048]]
        WCH2 = FA.ap()[:, 2048:3072]
        H4 = [FA.ap()[:, 0:2048],
              BIGB.ap()[:, 23040:27136].bitcast(F32)]
        gate = BIGB.ap()[:, 20992:22016].bitcast(F32)
        ev1 = [BIGB.ap()[:, 16896 + i * 1024: 16896 + (i + 1) * 1024].bitcast(F32)
               for i in range(2)]
        ev2 = [BIGB.ap()[:, 18944 + i * 1024: 18944 + (i + 1) * 1024].bitcast(F32)
               for i in range(2)]
        evf1 = [FA.ap()[:, 512 * i: 512 * (i + 1)] for i in range(2)]
        hstg = [BIGB.ap()[:, 22016 + 512 * i: 22016 + 512 * (i + 1)]
                for i in range(2)]
        evf2 = [FA.ap()[:, 1024 + 512 * i: 1024 + 512 * (i + 1)] for i in range(2)]

        xnfo = BIGA.ap()[:, 0:16384]
        xnbf = [BIGA.ap()[:, 16384 + i * 2048: 16384 + (i + 1) * 2048]
                for i in range(2)]
        xnT = [BIGA.ap()[:, 20480 + i * 2048: 20480 + (i + 1) * 2048]
               for i in range(2)]
        xqbf = BIGA.ap()[:, 24576:26624]

        KQ = BIGA.ap()
        VE_ = BIGB.ap()
        xnwin = BIGC.ap()[:, 0:4096]
        qpbuf = BIGC.ap()[:, 4096:4096 + 4096]
        futs_sb = BIGC.ap()[:, 8192:12288]
        mskb = BIGC.ap()[:, 14592:14848].bitcast(F32)
        wqch = BIGC.ap()[:, 14848:15872]
        tri = BIGC.ap()[:, 14336:14592].bitcast(F32)
        scm = BIGC.ap()[:, 12288:12800]
        pt = [BIGC.ap()[:, 12800:13312], BIGC.ap()[:, 13312:13824]]
        aon = BIGC.ap()[:, 13824:14336]
        wkq = BIGD.ap()[:, 0:8192]
        wvq = BIGD.ap()[:, 8192:16384]

        PS = [ctx.enter_context(nc.psum_tensor(f"PS{i}", [128, 1024], F32))
              for i in range(4)]
        PSbf = [p.ap().bitcast(BF16) for p in PS]

        blk = ctx.enter_context(nc.Block())

        op("gp", lambda g: g.memset(ident.ap(), 0.0))
        op("gp", lambda g: g.affine_select(
            out=ident.ap(), in_=ident.ap(), compare_op=ALU.not_equal, fill=1.0,
            base=0, pattern=[[-1, 128]], channel_multiplier=1))
        op("gp", lambda g: g.memset(ones_col.ap(), 1.0))
        op("gp", lambda g: g.memset(ones_row.ap(), 1.0))
        op("gp", lambda g: g.memset(onesf_col.ap(), 1.0))
        op("gp", lambda g: g.memset(onesf_row.ap(), 1.0))
        op("gp", lambda g: g.memset(accw.ap(), 0.0))
        op("gp", lambda g: g.memset(hmax.ap(), 0.0))
        op("gp", lambda g: g.dma_start(out=msk_sb[:], in_=msk), inc=("ldg", 16))

        def ln_stats(par, src, waits):
            for gi in range(4):
                op("ve", (lambda g0, s0: (lambda v: v.bn_stats(
                    bn6[:, g0 * 6:(g0 + 1) * 6],
                    s0[:, g0 * 512:(g0 + 1) * 512])))(gi, src),
                    waits=waits if gi == 0 else (),
                    inc=("ve", 1) if gi == 3 else None)
            e = now("ve")
            op("ve", lambda v: v.bn_aggr(mv2[:], bn6[:]), waits=[("ve", e)],
               inc=("ve", 1))
            e = now("ve")
            op("ve", (lambda pp: (lambda v: v.tensor_scalar(
                vv[pp][:], mv2[:, 1:2], EPS_LN, None, ALU.add)))(par),
               waits=[("ve", e)], inc=("ve", 1))
            e = now("ve")
            op("ac", (lambda pp: (lambda s: s.activation(
                sd[pp][:], vv[pp][:], AF.Sqrt)))(par),
               waits=[("ve", e)], inc=("ac", 1))
            e = now("ac")
            op("ve", (lambda pp: (lambda v: v.reciprocal(
                rstd[pp][:], sd[pp][:])))(par),
               waits=[("ac", e)], inc=("ve", 1))
            e = now("ve")
            op("ve", (lambda pp: (lambda v: v.scalar_tensor_tensor(
                nmr[pp][:], mv2[:, 0:1], -1.0, rstd[pp][:],
                ALU.mult, ALU.mult)))(par),
               waits=[("ve", e)], inc=("ve", 1))
            return now("ve")

        ac_guard = 0
        for k in range(TPC):
            par = k % 2
            e_ld = op("sy", (lambda kk, pp: (lambda s: s.dma_start(
                out=XO[pp], in_=x_own[kk * 128:(kk + 1) * 128, :])))(k, par),
                waits=[("ve", ac_guard)], inc=("ldx", 16))
            e_st = ln_stats(par, XO[par], [("ldx", e_ld)])
            op("ve", (lambda pp: (lambda v: v.tensor_scalar(
                XN32, XO[pp], rstd[pp][:, 0:1], nmr[pp][:, 0:1],
                ALU.mult, ALU.add)))(par),
                waits=[("ve", e_st)], inc=("ve", 1))
            op("ve", (lambda pp: (lambda v: v.tensor_scalar(
                xnbf[pp], XO[pp], rstd[pp][:, 0:1], nmr[pp][:, 0:1],
                ALU.mult, ALU.add)))(par),
                waits=[("pe", now("pe"))], inc=("ve", 1))
            ac_guard = now("ve")
            e_n = now("ve")
            op("ve", lambda v: v.reduce_max(am1[:], XN32, axis=AX.X,
                                            apply_absolute_value=True),
               waits=[("ve", e_n)], inc=("ve", 1))
            e = now("ve")
            op("ve", lambda v: v.tensor_scalar(am1[:], am1[:], EPS_Q, None,
                                               ALU.max),
               waits=[("ve", e)], inc=("ve", 1))
            e = now("ve")
            op("ve", (lambda kk: (lambda v: v.tensor_scalar(
                sx_sb[:, kk:kk + 1], am1[:], 1.0 / 127.0, None, ALU.mult)))(k),
               waits=[("ve", e)], inc=("ve", 1))
            e = now("ve")
            op("ve", lambda v: v.reciprocal(zr[:], am1[:]),
               waits=[("ve", e)], inc=("ve", 1))
            e = now("ve")
            op("ve", lambda v: v.tensor_scalar(zv[:], zr[:], 127.0, None,
                                               ALU.mult),
               waits=[("ve", e)], inc=("ve", 1))
            e = now("ve")
            op("ve", lambda v: v.tensor_scalar(T1S, XN32, zv[:, 0:1],
                                               MAGIC, ALU.mult, ALU.add),
               waits=[("ve", e)], inc=("ve", 1))
            e = now("ve")
            op("ve", lambda v: v.tensor_scalar(xqbf, T1S, MAGIC, None,
                                               ALU.subtract),
               waits=[("ve", e)], inc=("ve", 1))
            e_xq = now("ve")
            for which, (srcap, dstap) in enumerate(
                    ((xnbf[par], xnfo), (xqbf, BIGC.ap()))):
                egate_ve = e_n if which == 0 else e_xq
                for ic in range(16):
                    pp = ic % 2
                    e_pe = op("pe", (lambda s0, icc, ppp: (lambda t: t.transpose(
                        PSbf[ppp][:, 0:128], s0[:, icc * 128:(icc + 1) * 128],
                        ident[:])))(srcap, ic, pp),
                        waits=[("ve", max(egate_ve, now("ve")))],
                        inc=("pe", 1))
                    op("ve", (lambda d0, icc, kk, ppp: (lambda v: v.tensor_copy(
                        d0[:, icc * 1024 + kk * 128: icc * 1024 + (kk + 1) * 128],
                        PSbf[ppp][:, 0:128])))(dstap, ic, k, pp),
                        waits=[("pe", e_pe)], inc=("ve", 1))
            e_v = now("ve")
            op("sy", (lambda kk: (lambda s: s.dma_start(
                out=xq_d.rearrange("a p n -> p a n")[
                    :, :, kk * 128:(kk + 1) * 128],
                in_=BIGC.ap().rearrange("p (a n) -> p a n", a=16)[
                    :, :, kk * 128:(kk + 1) * 128])))(k),
                waits=[("ve", e_v)], inc=("stx", 16))
        E_A_spill = now("stx")

        e_prev_dma = [0, 0]
        for tt in range(NT):
            par = tt % 2
            e_ld = op("sy", (lambda t0, pp: (lambda s: s.dma_start(
                out=XO[pp], in_=x_full[t0 * 128:(t0 + 1) * 128, :])))(tt, par),
                waits=[("ve", now("ve"))], inc=("ldx", 16))
            e_st = ln_stats(par, XO[par], [("ldx", e_ld)])
            op("ve", (lambda pp: (lambda v: v.tensor_scalar(
                xnbf[pp], XO[pp], rstd[pp][:, 0:1], nmr[pp][:, 0:1],
                ALU.mult, ALU.add)))(par),
                waits=[("ve", e_st), ("pe", now("pe"))], inc=("ve", 1))
            e_n = now("ve")
            for ic in range(16):
                pp = ic % 2
                e_pe = op("pe", (lambda ppx, icc, ppp: (lambda t: t.transpose(
                    PSbf[ppp][:, 0:128], xnbf[ppx][:, icc * 128:(icc + 1) * 128],
                    ident[:])))(par, ic, pp),
                    waits=[("ve", e_n), ("stx", e_prev_dma[par]),
                           ("ve", now("ve"))],
                    inc=("pe", 1))
                op("ve", (lambda ppx, icc, ppp: (lambda v: v.tensor_copy(
                    xnT[ppx][:, icc * 128:(icc + 1) * 128],
                    PSbf[ppp][:, 0:128])))(par, ic, pp),
                    waits=[("pe", e_pe)], inc=("ve", 1))
            e_v = now("ve")
            e_prev_dma[par] = op("sy", (lambda t0, pp: (lambda s: s.dma_start(
                out=xnf_d[t0].rearrange("a p n -> p a n"),
                in_=xnT[pp].rearrange("p (a n) -> p a n", a=16))))(tt, par),
                waits=[("ve", e_v)], inc=("stx", 16))
        E_B = now("stx")

        for oc in range(16):
            par = oc % 2
            wslc = BIGD.ap()[:, par * 8192: par * 8192 + 2048]
            e_w = op("gp", (lambda occ, hf: (lambda g: g.dma_start(
                out=hf.rearrange("p (a n) -> p a n", a=16),
                in_=Wq.rearrange("(a p) n -> p a n", p=128)[
                    :, :, occ * 128:(occ + 1) * 128])))(oc, wslc),
                waits=[("pe", now("pe"))],
                inc=("ldg", 16))
            for th in range(2):
                pp = th
                e_mm = None
                for ic in range(16):
                    e_mm = op("pe", (lambda hf, icc, thh, ppp: (lambda t: t.matmul(
                        PS[2 + ppp][:, 0:512],
                        hf[:, icc * 128:(icc + 1) * 128],
                        xnfo[:, icc * 1024 + thh * 512: icc * 1024 + thh * 512 + 512],
                        start=(icc == 0), stop=(icc == 15))))(wslc, ic, th, pp),
                        waits=([("ldg", e_w), ("ve", now("ve"))]
                               if ic == 0 else ()),
                        inc=("pe", 1) if ic == 15 else None)
                op("ve", (lambda ppp: (lambda v: v.tensor_copy(
                    pt[ppp], PS[2 + ppp][:, 0:512])))(pp),
                    waits=[("pe", e_mm), ("stx", max(E_A_spill, now("stx") - 32))],
                    inc=("ve", 1))
                e_v = now("ve")
                op("sy", (lambda occ, thh, ppp: (lambda s: s.dma_start(
                    out=q_d[occ, :, thh * 512: thh * 512 + 512],
                    in_=pt[ppp])))(oc, th, pp),
                    waits=[("ve", e_v)], inc=("stx", 16))
        E_C = now("stx")

        wlist = [(wg, 0, 64), (wu, 1, 64), (wd, 2, 16)]
        wchunks = []
        for (w_ap, widx, ntile) in wlist:
            ncol = w_ap.shape[1] // 1024
            for t0 in range(ntile):
                for cc in range(ncol):
                    wchunks.append((w_ap, widx, t0, cc))
        assert len(wchunks) == 384

        wch_i = [0]

        def emit_scale_chunks(n):
            for _ in range(n):
                i = wch_i[0]
                if i >= len(wchunks):
                    return
                wch_i[0] += 1
                w_ap, widx, t0, cc = wchunks[i]
                par = i % 2
                e_d = op("sc", (lambda wa, tt0, ccc, pp: (lambda g: g.dma_start(
                    out=WCH[pp],
                    in_=wa[tt0 * 128:(tt0 + 1) * 128,
                           ccc * 1024:(ccc + 1) * 1024])))(w_ap, t0, cc, par),
                    waits=[("ve", now("ve"))], inc=("lds", 16))
                op("ve", (lambda pp: (lambda v: v.reduce_sum(
                    am1[:], WCH[pp], axis=AX.X,
                    apply_absolute_value=True)))(par),
                    waits=[("lds", e_d)], inc=("ve", 1))
                e = now("ve")
                op("ve", (lambda wi: (lambda v: v.tensor_add(
                    accw[:, wi:wi + 1], accw[:, wi:wi + 1], am1[:])))(widx),
                    waits=[("ve", e)], inc=("ve", 1))

        def emit_scale_final():
            for wi in range(3):
                e_pe = op("pe", (lambda w2: (lambda t: t.matmul(
                    PS[3][0:1, 512 + w2:512 + w2 + 1], accw[:, w2:w2 + 1],
                    onesf_col[:], start=True, stop=True)))(wi),
                    waits=[("ve", now("ve")), ("ac", now("ac"))],
                    inc=("pe", 1))
                op("ac", (lambda w2: (lambda s: s.copy(
                    tot_sb[0:1, w2:w2 + 1],
                    PS[3][0:1, 512 + w2:512 + w2 + 1])))(wi),
                    waits=[("pe", e_pe)], inc=("ac", 1))
            e = now("ac")
            op("ve", lambda v: v.tensor_scalar(tot_sb[:], tot_sb[:],
                                               1.0 / NW_EL, EPS_Q,
                                               ALU.mult, ALU.add),
               waits=[("ac", e)], inc=("ve", 1))
            e = now("ve")
            op("ve", lambda v: v.reciprocal(invs_sb[:], tot_sb[:]),
               waits=[("ve", e)], inc=("ve", 1))
            e_r = now("ve")
            for wi in range(3):
                for (src, dst) in ((invs_sb, invs_b), (tot_sb, sw_b)):
                    e_pe = op("pe", (lambda s0, w2: (lambda t: t.matmul(
                        PS[3][:, 512 + w2:512 + w2 + 1], onesf_row[:],
                        s0[0:1, w2:w2 + 1], start=True, stop=True)))(src, wi),
                        waits=[("ve", e_r), ("ve", now("ve"))],
                        inc=("pe", 1))
                    op("ve", (lambda d0, w2: (lambda v: v.tensor_copy(
                        d0[:, w2:w2 + 1],
                        PS[3][:, 512 + w2:512 + w2 + 1])))(dst, wi),
                        waits=[("pe", e_pe)], inc=("ve", 1))

        wq_i = [0]
        wd_st = [0]
        wq_dsts = {0: wgq_d, 1: wuq_d, 2: wdq_d}

        def emit_quant_chunks(n):
            for _ in range(n):
                i = wq_i[0]
                if i >= len(wchunks):
                    return
                wq_i[0] += 1
                w_ap, widx, t0, cc = wchunks[i]
                par = i % 2
                e_d = op("sc", (lambda wa, tt0, ccc, pp: (lambda g: g.dma_start(
                    out=WCH[pp],
                    in_=wa[tt0 * 128:(tt0 + 1) * 128,
                           ccc * 1024:(ccc + 1) * 1024])))(w_ap, t0, cc, par),
                    waits=[("ve", now("ve"))], inc=("lds", 16))
                op("ve", (lambda wi2, pp: (lambda v: v.tensor_scalar(
                    WCH2, WCH[pp], invs_b[:, wi2:wi2 + 1], 1.0,
                    ALU.mult, ALU.min)))(widx, par),
                    waits=[("lds", e_d), ("po", now("po")),
                           ("lds", wd_st[0])], inc=("ve", 1))
                e = now("ve")
                op("po", (lambda pp: (lambda g: g.tensor_scalar(
                    WCH[pp], WCH2, -1.0, MAGIC, ALU.max, ALU.add)))(par),
                    waits=[("ve", e)], inc=("po", 1))
                e_p = now("po")
                op("ve", (lambda pp: (lambda v: v.tensor_scalar(
                    wqch, WCH[pp], MAGIC, None, ALU.subtract)))(par),
                    waits=[("po", e_p), ("lds", now("lds"))], inc=("ve", 1))
                e = now("ve")
                if widx < 2:
                    op("sc", (lambda wi2, tt0, ccc: (lambda g: g.dma_start(
                        out=wq_dsts[wi2][tt0 * 128:(tt0 + 1) * 128,
                                         ccc * 1024:(ccc + 1) * 1024],
                        in_=wqch)))(widx, t0, cc),
                        waits=[("ve", e)], inc=("lds", 16))
                else:
                    stgbf = WCH2.bitcast(BF16)
                    for b in range(8):
                        ppb = b % 2
                        e_tp = op("pe", (lambda bb, pb: (lambda t: t.transpose(
                            PSbf[3][:, pb * 1024: pb * 1024 + 128],
                            wqch[:, bb * 128:(bb + 1) * 128],
                            ident[:])))(b, ppb),
                            waits=[("ve", now("ve"))], inc=("pe", 1))
                        op("ve", (lambda bb, pb: (lambda v: v.tensor_copy(
                            stgbf[:, bb * 128:(bb + 1) * 128],
                            PSbf[3][:, pb * 1024: pb * 1024 + 128])))(b, ppb),
                            waits=[("pe", e_tp)], inc=("ve", 1))
                    e2 = now("ve")
                    o8, obb = t0 // 2, t0 % 2
                    wd_st[0] = op("sc", (lambda oo8, oob, ccc2: (lambda g: g.dma_start(
                        out=wdqT_d.rearrange("a p c n -> p a c n")[
                            :, oo8, ccc2 * 8:(ccc2 + 1) * 8,
                            oob * 128: oob * 128 + 128],
                        in_=stgbf[:, 0:1024].rearrange(
                            "p (b f) -> p b f", b=8))))(o8, obb, cc),
                        waits=[("ve", e2)], inc=("lds", 16))

        RHS0 = [0, 129, 258, 387]
        AOOF = [0, 129, 512, 641]
        ZREL = [0, 128, 0, 128]
        AOREL = [1, 0, 1, 0]

        op("gp", lambda g: g.iota(tri, pattern=[[-1, 128]], base=0,
                                  channel_multiplier=1,
                                  allow_small_or_imprecise_dtypes=True),
           waits=[("stx", E_A_spill)], inc=("po", 1))

        e_aon_dma = 0
        for qp in range(4):
            e_wk = op("gp", (lambda qq: (lambda g: g.dma_start(
                out=wkq.rearrange("p (a n) -> p a n", a=16),
                in_=Wk.rearrange("(a p) n -> p a n", p=128)[
                    :, :, qq * 512:(qq + 1) * 512])))(qp),
                waits=[("pe", now("pe"))], inc=("ldg", 16))
            e_wv = op("gp", (lambda qq: (lambda g: g.dma_start(
                out=wvq.rearrange("p (a n) -> p a n", a=16),
                in_=Wv.rearrange("(a p) n -> p a n", p=128)[
                    :, :, qq * 512:(qq + 1) * 512])))(qp),
                inc=("ldg", 16))
            op("gp", lambda g: g.memset(
                VE_.rearrange("p (c b) -> p c b", b=516)[:, :, 0:1], 1.0),
                waits=[("pe", now("pe"))])
            op("gp", lambda g: g.memset(
                VE_.rearrange("p (c b) -> p c b", b=516)[:, :, 257:259], 1.0))
            op("gp", lambda g: g.memset(
                VE_.rearrange("p (c b) -> p c b", b=516)[:, :, 515:516], 1.0),
                inc=("po", 1))
            e_ones = now("po")

            for tw in range(32):
                e_x = None
                for s2 in range(2):
                    e_x = op("sy", (lambda tww, ss2: (lambda s: s.dma_start(
                        out=xnwin.rearrange("p (a n2) -> p a n2", n2=256)[
                            :, :, ss2 * 128:(ss2 + 1) * 128],
                        in_=xnf_d[2 * tww + ss2].rearrange("a p n -> p a n"))))(
                            tw, s2),
                        waits=[("stx", E_B), ("pe", now("pe"))]
                        if s2 == 0 else (),
                        inc=("ldx", 16))
                last_k = None
                last_v = None
                for ic in range(16):
                    for oc4 in range(4):
                        last_k = op("pe", (lambda icc, o4: (lambda t: t.matmul(
                            PS[0][:, o4 * 256:(o4 + 1) * 256],
                            wkq[:, icc * 512 + o4 * 128:
                                icc * 512 + o4 * 128 + 128],
                            xnwin[:, icc * 256: icc * 256 + 256],
                            start=(icc == 0 and o4 % 2 == 0),
                            stop=(icc == 15))))(ic, oc4),
                            waits=([("ldx", e_x), ("ldg", e_wv),
                                    ("ve", now("ve"))]
                                   if (ic == 0 and oc4 == 0) else ()),
                            inc=("pe", 1) if (ic == 15 and oc4 == 3) else None)
                    for st2 in range(2):
                        last_v = op("pe", (lambda icc, s2: (lambda t: t.matmul(
                            PS[1][:, s2 * 512:(s2 + 1) * 512],
                            xnwin[:, icc * 256 + s2 * 128:
                                  icc * 256 + s2 * 128 + 128],
                            wvq[:, icc * 512: icc * 512 + 512],
                            start=(icc == 0), stop=(icc == 15))))(ic, st2),
                            inc=("pe", 1) if (ic == 15 and st2 == 1) else None)
                for oc4 in range(4):
                    op("ve", (lambda tww, o4: (lambda v: v.tensor_copy(
                        KQ[:, o4 * T + tww * 256: o4 * T + tww * 256 + 256],
                        PS[0][:, o4 * 256:(o4 + 1) * 256])))(tw, oc4),
                        waits=[("pe", last_k), ("po", e_ones)]
                        if oc4 == 0 else (),
                        inc=("ve", 1))
                for st2 in range(2):
                    ch = tw * 2 + st2
                    for half in range(2):
                        op("ve", (lambda c0, s2, hh: (lambda v: v.tensor_copy(
                            VE_[:, c0 * 516 + (1 if hh == 0 else 259):
                                c0 * 516 + (1 if hh == 0 else 259) + 256],
                            PS[1][:, s2 * 512 + hh * 256:
                                  s2 * 512 + hh * 256 + 256])))(ch, st2, half),
                            waits=[("pe", last_v)]
                            if (st2 == 0 and half == 0) else (),
                            inc=("ve", 1))
            E_kv = now("ve")

            op("gp", lambda g: g.memset(futs_sb[:, 7 * 512:8 * 512], 0.0),
               waits=[("ac", now("ac"))], inc=("po", 1))
            e_snap = now("ac")
            first = True
            e_fm = None
            for j in range(NT - 1, 7, -1):
                for half in range(2):
                    e_fm = op("pe", (lambda jj, hh, ff: (lambda t: t.matmul(
                        PS[3][0:1, hh * 256: hh * 256 + 256],
                        ones_col[:],
                        VE_[:, jj * 516 + (1 if hh == 0 else 259):
                            jj * 516 + (1 if hh == 0 else 259) + 256],
                        start=ff, stop=False)))(j, half, first and half == 0),
                        waits=([("ve", E_kv), ("ac", e_snap)]
                               if half == 0 else ()),
                        inc=("pe", 1) if half == 1 else None)
                first = False
                if j % 8 == 0:
                    kcls = j // 8 - 1
                    op("ac", (lambda kc: (lambda s: s.copy(
                        futs_sb[0:1, kc * 512:(kc + 1) * 512],
                        PS[3][0:1, 0:512])))(kcls),
                        waits=[("pe", e_fm)], inc=("ac", 1))
                    e_snap = now("ac")
            E_futs = now("ac")

            e_q = op("sy", (lambda qq: (lambda s: s.dma_start(
                out=qpbuf.rearrange("p (a n) -> p a n", a=4),
                in_=q_d[qq * 4:(qq + 1) * 4].rearrange("a p n -> p a n"))))(qp),
                waits=[("stx", E_C), ("pe", now("pe")), ("po", e_ones)],
                inc=("ldx", 16))

            slot_base = 0
            for k8 in range(TPC):
                nslots = 8 * k8 + 8
                aops = PS[k8 % 2]
                e_ao_free = now("ve")
                last_pv = None
                for j in range(nslots):
                    spar = j % 2
                    e_qk = None
                    for hh in range(4):
                        wts = ()
                        if hh == 0:
                            if j == 0:
                                wts = [("ldx", e_q), ("ac", now("ac")),
                                       ("ve", e_ao_free)]
                            else:
                                wts = [("ac", max(0, now("ac") - 1)),
                                       ("ve", now("ve"))]
                        e_qk = op("pe", (lambda jj, h2, kk8, sp: (lambda t: t.matmul(
                            PS[2][:, sp * 512 + h2 * 128:
                                  sp * 512 + h2 * 128 + 128],
                            KQ[:, h2 * T + jj * 128: h2 * T + jj * 128 + 128],
                            qpbuf[:, h2 * 1024 + kk8 * 128:
                                  h2 * 1024 + kk8 * 128 + 128],
                            start=True, stop=True)))(j, hh, k8, spar),
                            waits=wts,
                            inc=("pe", 1) if hh == 3 else None)
                    masked = (j >= nslots - 8)
                    if masked:
                        slot = slot_base + j
                        op("ve", (lambda sl: (lambda v: v.tensor_scalar(
                            mskb, tri, msk_sb[:, sl:sl + 1], None,
                            ALU.is_le)))(slot),
                            waits=[("pe", e_qk)], inc=("ve", 1))
                        e_m = None
                        for hh in range(4):
                            e_m = op("ve", (lambda h2, sp: (lambda v: v.tensor_mul(
                                scm[:, h2 * 128:(h2 + 1) * 128],
                                PS[2][:, sp * 512 + h2 * 128:
                                      sp * 512 + h2 * 128 + 128],
                                mskb)))(hh, spar),
                                inc=("ve", 1))
                        e_exp = op("ac", (lambda sp: (lambda s: s.activation(
                            pt[sp], scm, AF.Exp, scale=INV_TAU)))(spar),
                            waits=[("ve", e_m)], inc=("ac", 1))
                    else:
                        e_exp = op("ac", (lambda sp: (lambda s: s.activation(
                            pt[sp], PS[2][:, sp * 512: sp * 512 + 512],
                            AF.Exp, scale=INV_TAU)))(spar),
                            waits=[("pe", e_qk)], inc=("ac", 1))
                    for hh in range(4):
                        last_pv = op("pe", (lambda jj, h2, sp, k9: (lambda t: t.matmul(
                            k9[:, AOOF[h2]: AOOF[h2] + 129],
                            pt[sp][:, h2 * 128:(h2 + 1) * 128],
                            VE_[:, jj * 516 + RHS0[h2]:
                                jj * 516 + RHS0[h2] + 129],
                            start=(jj == 0 and h2 % 2 == 0),
                            stop=False)))(j, hh, spar, aops),
                            waits=[("ac", e_exp)] if hh == 0 else (),
                            inc=("pe", 1) if hh == 3 else None)
                if k8 < 7:
                    for hh in range(4):
                        last_pv = op("pe", (lambda h2, kk8, k9: (lambda t: t.matmul(
                            k9[:, AOOF[h2] + AOREL[h2]:
                               AOOF[h2] + AOREL[h2] + 128],
                            ones_row[:],
                            futs_sb[0:1, kk8 * 512 + h2 * 128:
                                    kk8 * 512 + h2 * 128 + 128],
                            start=False, stop=(h2 == 3))))(hh, k8, aops),
                            waits=[("ac", E_futs)] if hh == 0 else (),
                            inc=("pe", 1) if hh == 3 else None)
                futcnt = float(128 * (NT - nslots))
                for hh in range(4):
                    e1 = op("ve", (lambda h2, fc, k9: (lambda v: v.tensor_scalar(
                        zv[:], k9[:, AOOF[h2] + ZREL[h2]:
                                  AOOF[h2] + ZREL[h2] + 1], fc, None,
                        ALU.add)))(hh, futcnt, aops),
                        waits=([("pe", last_pv), ("stx", e_aon_dma)]
                               if hh == 0 else [("ve", now("ve"))]),
                        inc=("ve", 1))
                    op("ve", lambda v: v.reciprocal(zr[:], zv[:]),
                       waits=[("ve", e1)], inc=("ve", 1))
                    e2 = now("ve")
                    op("ve", (lambda h2, k9: (lambda v: v.tensor_scalar(
                        aon[:, h2 * 128:(h2 + 1) * 128],
                        k9[:, AOOF[h2] + AOREL[h2]:
                           AOOF[h2] + AOREL[h2] + 128],
                        zr[:, 0:1], None, ALU.mult)))(hh, aops),
                        waits=[("ve", e2)], inc=("ve", 1))
                e_v = now("ve")
                e_aon_dma = op("sy", (lambda kk8, qq: (lambda s: s.dma_start(
                    out=ao_d[kk8 * 128:(kk8 + 1) * 128,
                             qq * 512:(qq + 1) * 512],
                    in_=aon)))(k8, qp),
                    waits=[("ve", e_v)], inc=("stx", 16))
                slot_base += nslots

            if qp == 0:
                emit_scale_chunks(192)
            elif qp == 1:
                emit_scale_chunks(192)
                emit_scale_final()
            elif qp == 2:
                emit_quant_chunks(192)
            else:
                emit_quant_chunks(192)
        emit_quant_chunks(384)
        E_D = now("stx")
        E_wq = now("lds")

        e_xq = op("sy", lambda s: s.dma_start(
            out=BIGC.ap().rearrange("p (a n) -> p a n", a=16),
            in_=xq_d.rearrange("a p n -> p a n")),
            waits=[("stx", E_A_spill), ("pe", now("pe")),
                   ("ve", now("ve"))],
            inc=("ldx", 16))
        xqf = BIGC.ap()
        e_pre = now("ve")
        for k in range(TPC):
            op("ve", (lambda kk: (lambda v: v.tensor_scalar(
                sxg_sb[:, kk:kk + 1], sx_sb[:, kk:kk + 1], sw_b[:, 0:1],
                None, ALU.mult)))(k),
                waits=[("ve", e_pre)] if k == 0 else (), inc=("ve", 1))
            op("ve", (lambda kk: (lambda v: v.tensor_scalar(
                su_sb[:, kk:kk + 1], sx_sb[:, kk:kk + 1], sw_b[:, 1:2],
                None, ALU.mult)))(k), inc=("ve", 1))
        E_sc = now("ve")

        e_h_dma = {}
        for opan in range(32):
            par = opan % 2
            wgT = BIGD.ap()[:, par * 8192: par * 8192 + 4096]
            wuT = BIGD.ap()[:, par * 8192 + 4096: par * 8192 + 8192]
            e_g = None
            e_u = None
            for ic in range(16):
                e_g = op("sy", (lambda icc, opp, wb: (lambda s: s.dma_start(
                    out=wb[:, icc * 256: icc * 256 + 256],
                    in_=wgq_d[opp * 256:(opp + 1) * 256,
                              icc * 128:(icc + 1) * 128],
                    transpose=True)))(ic, opan, wgT),
                    waits=([("lds", E_wq), ("pe", now("pe")),
                            ("ldx", e_xq)] if ic == 0 else ()),
                    inc=("ldx", 16))
                e_u = op("sy", (lambda icc, opp, wb: (lambda s: s.dma_start(
                    out=wb[:, icc * 256: icc * 256 + 256],
                    in_=wuq_d[opp * 256:(opp + 1) * 256,
                              icc * 128:(icc + 1) * 128],
                    transpose=True)))(ic, opan, wuT),
                    waits=([("lds", E_wq), ("pe", now("pe")),
                            ("ac", now("ac"))] if ic == 0 else ()),
                    inc=("ldx", 16))
            for tk in range(TPC):
                ppar = tk % 2
                e_mm = None
                for ic in range(16):
                    op("pe", (lambda icc, tkk, wb, pq: (lambda t: t.matmul(
                        PS[pq][:, 0:256],
                        xqf[:, icc * 1024 + tkk * 128:
                            icc * 1024 + tkk * 128 + 128],
                        wb[:, icc * 256: icc * 256 + 256],
                        start=(icc == 0), stop=(icc == 15))))(ic, tk, wgT, ppar),
                        waits=([("ldx", e_g), ("ldx", e_u),
                                ("ac", now("ac")), ("ve", now("ve"))]
                               if ic == 0 else ()),
                        inc=("pe", 1) if ic == 15 else None)
                    e_mm = op("pe", (lambda icc, tkk, wb, pq: (lambda t: t.matmul(
                        PS[pq][:, 512:768],
                        xqf[:, icc * 1024 + tkk * 128:
                            icc * 1024 + tkk * 128 + 128],
                        wb[:, icc * 256: icc * 256 + 256],
                        start=(icc == 0), stop=(icc == 15))))(ic, tk, wuT, ppar),
                        inc=("pe", 1) if ic == 15 else None)
                e_sig = op("ac", (lambda tkk, pq: (lambda s: s.activation(
                    gate[:, 0:256], PS[pq][:, 0:256], AF.Sigmoid,
                    scale=sxg_sb[:, tkk:tkk + 1])))(tk, ppar),
                    waits=[("pe", e_mm), ("ve", E_sc),
                           ("ve", now("ve"))], inc=("ac", 1))
                hpar = (opan * 8 + tk) % 2
                e_h = op("ve", (lambda pq, hp: (lambda v: v.tensor_mul(
                    hstg[hp][:, 0:256], gate[:, 0:256],
                    PS[pq][:, 512:768])))(ppar, hpar),
                    waits=[("ac", e_sig),
                           ("stx", e_h_dma.get(hpar, 0))],
                    inc=("ve", 1))
                op("ve", (lambda hp: (lambda v: v.reduce_max(
                    am1[:], hstg[hp][:, 0:256], axis=AX.X,
                    apply_absolute_value=True)))(hpar),
                    waits=[("ve", e_h)], inc=("ve", 1))
                e = now("ve")
                op("ve", (lambda tkk: (lambda v: v.tensor_max(
                    hmax[:, tkk:tkk + 1], hmax[:, tkk:tkk + 1], am1[:])))(tk),
                    waits=[("ve", e)], inc=("ve", 1))
                e_v = now("ve")
                e_h_dma[hpar] = op("sy", (lambda tkk, opp, hp: (lambda s: s.dma_start(
                    out=h_d[tkk * 128:(tkk + 1) * 128,
                            opp * 256:(opp + 1) * 256],
                    in_=hstg[hp][:, 0:256])))(tk, opan, hpar),
                    waits=[("ve", e_v)], inc=("stx", 16))
        E_E1 = now("stx")

        e0 = now("ve")
        for tk in range(TPC):
            op("ve", (lambda kk: (lambda v: v.tensor_scalar(
                zv[:], hmax[:, kk:kk + 1], su_sb[:, kk:kk + 1], None,
                ALU.mult)))(tk),
                waits=[("ve", e0), ("stx", E_E1)] if tk == 0
                else [("ve", now("ve"))],
                inc=("ve", 1))
            e = now("ve")
            op("ve", (lambda kk: (lambda v: v.tensor_scalar(
                mc_sb[:, kk:kk + 1], zv[:], EPS_Q, None, ALU.max)))(tk),
                waits=[("ve", e)], inc=("ve", 1))
            e = now("ve")
            op("ve", (lambda kk: (lambda v: v.reciprocal(
                zr[:], mc_sb[:, kk:kk + 1])))(tk),
                waits=[("ve", e)], inc=("ve", 1))
            e = now("ve")
            op("ve", (lambda kk: (lambda v: v.tensor_scalar(
                zv[:], zr[:], su_sb[:, kk:kk + 1], None, ALU.mult)))(tk),
                waits=[("ve", e)], inc=("ve", 1))
            e = now("ve")
            op("ve", (lambda kk: (lambda v: v.tensor_scalar(
                qsc_sb[:, kk:kk + 1], zv[:], 127.0, None, ALU.mult)))(tk),
                waits=[("ve", e)], inc=("ve", 1))
            e = now("ve")
            op("ve", (lambda kk: (lambda v: v.tensor_scalar(
                zv[:], mc_sb[:, kk:kk + 1], 1.0 / 127.0, None,
                ALU.mult)))(tk), waits=[("ve", e)], inc=("ve", 1))
            e = now("ve")
            op("ve", (lambda kk: (lambda v: v.tensor_scalar(
                dsc_sb[:, kk:kk + 1], zv[:], sw_b[:, 2:3], None,
                ALU.mult)))(tk), waits=[("ve", e)], inc=("ve", 1))
        E_qs = now("ve")

        hload = BIGC.ap()[:, 0:8192]
        hT = [BIGB.ap()[:, i * 8192:(i + 1) * 8192] for i in range(2)]
        for grp in range(4):
            tset = [grp * 2, grp * 2 + 1]
            for ti, tk in enumerate(tset):
                e_hl = op("sy", (lambda tkk: (lambda s: s.dma_start(
                    out=hload, in_=h_d[tkk * 128:(tkk + 1) * 128, :])))(tk),
                    waits=[("stx", E_E1), ("pe", now("pe")),
                           ("ve", now("ve"))],
                    inc=("ldx", 16))
                for hf in range(4):
                    op("ve", (lambda tkk, h2: (lambda v: v.tensor_scalar(
                        H4[h2 % 2], hload[:, h2 * 2048: h2 * 2048 + 2048],
                        qsc_sb[:, tkk:tkk + 1], MAGIC, ALU.mult,
                        ALU.add)))(tk, hf),
                        waits=[("ldx", e_hl), ("ve", E_qs)]
                        if hf == 0 else [("ve", now("ve"))],
                        inc=("ve", 1))
                    e = now("ve")
                    op("ve", (lambda h2: (lambda v: v.tensor_scalar(
                        hload[:, h2 * 2048: h2 * 2048 + 2048], H4[h2 % 2],
                        MAGIC, None, ALU.subtract)))(hf),
                        waits=[("ve", e)], inc=("ve", 1))
                e_hq = now("ve")
                for ffc in range(64):
                    pp = ffc % 2
                    e_pe = op("pe", (lambda fc, ppp: (lambda t: t.transpose(
                        PSbf[2 + ppp][:, 0:128],
                        hload[:, fc * 128: fc * 128 + 128],
                        ident[:])))(ffc, pp),
                        waits=[("ve", e_hq if ffc < 2 else now("ve"))],
                        inc=("pe", 1))
                    op("ve", (lambda ti2, fc, ppp: (lambda v: v.tensor_copy(
                        hT[ti2][:, fc * 128: fc * 128 + 128],
                        PSbf[2 + ppp][:, 0:128])))(ti, ffc, pp),
                        waits=[("pe", e_pe)], inc=("ve", 1))
            E_hT = now("ve")
            for op8 in range(8):
                wdT = BIGA.ap()[:, (op8 % 2) * 16384:(op8 % 2) * 16384 + 16384]
                e_w = op("sy", (lambda o8, wb: (lambda s: s.dma_start(
                    out=wb[:, 0:16384],
                    in_=wdqT_d[o8].rearrange("p c n -> p (c n)"))))(op8, wdT),
                    waits=[("pe", now("pe")), ("ve", E_hT), ("lds", E_wq)],
                    inc=("ldx", 16))
                for ti, tk in enumerate(tset):
                    ppar = ti % 2
                    e_mm = None
                    for ffc in range(64):
                        e_mm = op("pe", (lambda ti2, fc, pq, wb: (lambda t: t.matmul(
                            PS[pq][:, 0:256],
                            hT[ti2][:, fc * 128: fc * 128 + 128],
                            wb[:, fc * 256: fc * 256 + 256],
                            start=(fc == 0), stop=(fc == 63))))(ti, ffc, ppar, wdT),
                            waits=([("ldx", e_w), ("ve", now("ve"))]
                                   if ffc == 0 else ()),
                            inc=("pe", 1) if ffc == 63 else None)
                    e_x2 = op("sy", (lambda tkk, o8, pq: (lambda s: s.dma_start(
                        out=ev1[pq][:, 0:256],
                        in_=x_own[tkk * 128:(tkk + 1) * 128,
                                  o8 * 256:(o8 + 1) * 256])))(tk, op8, ppar),
                        waits=[("ve", now("ve"))], inc=("ldx", 16))
                    op("ve", (lambda tkk, pq: (lambda v: v.scalar_tensor_tensor(
                        ev2[pq][:, 0:256], PS[pq][:, 0:256],
                        dsc_sb[:, tkk:tkk + 1], ev1[pq][:, 0:256],
                        ALU.mult, ALU.add)))(tk, ppar),
                        waits=[("pe", e_mm), ("ldx", e_x2),
                               ("stx", now("stx") - 32)],
                        inc=("ve", 1))
                    e_v = now("ve")
                    op("sy", (lambda tkk, o8, pq: (lambda s: s.dma_start(
                        out=res_d[tkk * 128:(tkk + 1) * 128,
                                  o8 * 256:(o8 + 1) * 256],
                        in_=ev2[pq][:, 0:256])))(tk, op8, ppar),
                        waits=[("ve", e_v)], inc=("stx", 16))
        E_res = now("stx")

        aof = BIGC.ap()
        e_ao = None
        for ic in range(16):
            e_ao = op("sy", (lambda icc: (lambda s: s.dma_start(
                out=aof[:, icc * 1024:(icc + 1) * 1024],
                in_=ao_d[:, icc * 128:(icc + 1) * 128],
                transpose=True)))(ic),
                waits=([("stx", E_D), ("pe", now("pe")),
                        ("ldx", now("ldx"))] if ic == 0 else ()),
                inc=("ldx", 16))
        for woh in range(2):
            wo_sb = BIGB.ap()[:, woh * 16384: woh * 16384 + 16384]
            e_wo = op("gp", (lambda wh, wb: (lambda g: g.dma_start(
                out=wb.rearrange("p (a n) -> p a n", a=16),
                in_=Wo.rearrange("(a p) n -> p a n", p=128)[
                    :, :, wh * 1024:(wh + 1) * 1024])))(woh, wo_sb),
                waits=[("pe", now("pe")), ("ve", now("ve"))],
                inc=("ldg", 16))
            for tk in range(TPC):
                for op2 in range(2):
                    ppar = (tk * 2 + op2) % 2
                    e_mm = None
                    for ic in range(16):
                        e_mm = op("pe", (lambda icc, tkk, o2, pq, wb: (lambda t: t.matmul(
                            PS[pq][:, 0:512],
                            aof[:, icc * 1024 + tkk * 128:
                                icc * 1024 + tkk * 128 + 128],
                            wb[:, icc * 1024 + o2 * 512:
                               icc * 1024 + o2 * 512 + 512],
                            start=(icc == 0), stop=(icc == 15))))(
                                ic, tk, op2, ppar, wo_sb),
                            waits=([("ldx", e_ao), ("ldg", e_wo),
                                    ("ve", now("ve"))] if ic == 0 else ()),
                            inc=("pe", 1) if ic == 15 else None)
                    oc0 = woh * 1024 + op2 * 512
                    e_r = op("sy", (lambda tkk, o0, pq: (lambda s: s.dma_start(
                        out=evf1[pq],
                        in_=res_d[tkk * 128:(tkk + 1) * 128,
                                  o0:o0 + 512])))(tk, oc0, ppar),
                        waits=[("stx", E_res), ("ve", now("ve"))],
                        inc=("ldx", 16))
                    op("ve", (lambda pq: (lambda v: v.tensor_add(
                        evf2[pq], PS[pq][:, 0:512], evf1[pq])))(ppar),
                        waits=[("pe", e_mm), ("ldx", e_r),
                               ("stx", now("stx") - 32)],
                        inc=("ve", 1))
                    e_v = now("ve")
                    op("sy", (lambda tkk, o0, pq: (lambda s: s.dma_start(
                        out=Y[tkk * 128:(tkk + 1) * 128, o0:o0 + 512],
                        in_=evf2[pq])))(tk, oc0, ppar),
                        waits=[("ve", e_v)], inc=("stx", 16))
        E_Y = now("stx")

        GROUPS = {"sy": ("sy",), "scl": ("ac", "sc"), "ve": ("ve",),
                  "pe": ("pe",), "gps": ("gp", "po")}

        def emit_engine(eng_obj, names):
            last_wait = {}
            for (eng, waits, fn, inc) in OPS:
                if eng not in names:
                    continue
                for (s_name, val) in waits:
                    if last_wait.get(s_name, -1) >= val:
                        continue
                    eng_obj.wait_ge(S[s_name], val)
                    last_wait[s_name] = val
                ins = fn(eng_obj)
                if inc is not None:
                    ins.then_inc(S[inc[0]], inc[1])
            if "sy" in names:
                eng_obj.wait_ge(S["stx"], E_Y)

        blk.sync(lambda e: emit_engine(e, GROUPS["sy"]))
        blk.scalar(lambda e: emit_engine(e, GROUPS["scl"]))
        blk.vector(lambda e: emit_engine(e, GROUPS["ve"]))
        blk.tensor(lambda e: emit_engine(e, GROUPS["pe"]))
        blk.gpsimd(lambda e: emit_engine(e, GROUPS["gps"]))

    return nc


_NC_CACHE = {}


def kernel(L0_attn_x, L0_attn_Wq, L0_attn_Wk, L0_attn_Wv, L0_attn_Wo,
           L0_ffn_w_gate, L0_ffn_w_up, L0_ffn_w_down):
    x = np.ascontiguousarray(np.asarray(L0_attn_x, np.float32)[0])
    Wq_ = np.ascontiguousarray(np.asarray(L0_attn_Wq, np.float32))
    Wk_ = np.ascontiguousarray(np.asarray(L0_attn_Wk, np.float32))
    Wv_ = np.ascontiguousarray(np.asarray(L0_attn_Wv, np.float32))
    Wo_ = np.ascontiguousarray(np.asarray(L0_attn_Wo, np.float32))
    wg_ = np.ascontiguousarray(np.asarray(L0_ffn_w_gate, np.float32))
    wu_ = np.ascontiguousarray(np.asarray(L0_ffn_w_up, np.float32))
    wd_ = np.ascontiguousarray(np.asarray(L0_ffn_w_down, np.float32))

    if "nc" not in _NC_CACHE:
        _NC_CACHE["nc"] = build_program()
    nc = _NC_CACHE["nc"]

    in_maps = []
    for c in range(8):
        tl = tiles_of_core(c)
        rows = np.concatenate([np.arange(g * 128, (g + 1) * 128) for g in tl])
        x_own = np.ascontiguousarray(x[rows])
        cols = []
        for k in range(TPC):
            g = tl[k]
            for j in range(8 * k + 8):
                cols.append((g - j) * 128.0)
        mvec = np.asarray(cols, np.float32)
        mskm = np.broadcast_to(mvec, (128, 288)).copy()
        in_maps.append({
            "x_full": x, "x_own": x_own, "msk": mskm,
            "Wq": Wq_, "Wk": Wk_, "Wv": Wv_, "Wo": Wo_,
            "wg": wg_, "wu": wu_, "wd": wd_,
        })

    res = run_bass_kernel_spmd(nc, in_maps, core_ids=list(range(8)))

    out = np.empty((T, D), np.float32)
    for c in range(8):
        y = res.results[c]["Y"]
        tl = tiles_of_core(c)
        for k, g in enumerate(tl):
            out[g * 128:(g + 1) * 128] = y[k * 128:(k + 1) * 128]
    return out[None]


# version 04 (final)
# speedup vs baseline: 1.4387x; 1.4387x over previous
from collections import defaultdict

import numpy as np
import concourse.bass as bass
import concourse.mybir as mybir
from concourse.bass_utils import run_bass_kernel_spmd

F32 = mybir.dt.float32
BF16 = mybir.dt.bfloat16
AF = mybir.ActivationFunctionType
ALU = mybir.AluOpType
AX = mybir.AxisListType

T, D, H, DH = 8192, 2048, 16, 128
NT = T // 128
TPC = 8
OWN = TPC * 128
DFF = 8192
MAGIC = 12582912.0
INV_TAU = 1.0 / float(np.float32(np.sqrt(np.float32(T))))
EPS_LN = 1e-5
EPS_Q = 1e-5
NW_EL = float(DFF * D)


def tiles_of_core(c):
    return [8 * k + (c + k) % 8 for k in range(TPC)]


def build_program():
    nc = bass.Bass("TRN2", target_bir_lowering=False,
                   dynamic_dma_scratch_size=16384)

    x_full = nc.dram_tensor("x_full", [T, D], F32, kind="ExternalInput").ap()
    x_own = nc.dram_tensor("x_own", [OWN, D], F32, kind="ExternalInput").ap()
    msk = nc.dram_tensor("msk", [128, 288], F32, kind="ExternalInput").ap()
    Wq = nc.dram_tensor("Wq", [D, D], F32, kind="ExternalInput").ap()
    Wk = nc.dram_tensor("Wk", [D, D], F32, kind="ExternalInput").ap()
    Wv = nc.dram_tensor("Wv", [D, D], F32, kind="ExternalInput").ap()
    Wo = nc.dram_tensor("Wo", [D, D], F32, kind="ExternalInput").ap()
    wg = nc.dram_tensor("wg", [DFF, D], F32, kind="ExternalInput").ap()
    wu = nc.dram_tensor("wu", [DFF, D], F32, kind="ExternalInput").ap()
    wd = nc.dram_tensor("wd", [D, DFF], F32, kind="ExternalInput").ap()
    Y = nc.dram_tensor("Y", [OWN, D], F32, kind="ExternalOutput").ap()

    xnf_d = nc.dram_tensor("xnf_d", [NT, 16, 128, 128], BF16).ap()
    q_d = nc.dram_tensor("q_d", [16, 128, OWN], BF16).ap()
    xq_d = nc.dram_tensor("xq_d", [16, 128, OWN], BF16).ap()
    ao_d = nc.dram_tensor("ao_d", [OWN, D], BF16).ap()
    res_d = nc.dram_tensor("res_d", [OWN, D], F32).ap()
    h_d = nc.dram_tensor("h_d", [OWN, DFF], BF16).ap()
    wgq_d = nc.dram_tensor("wgq_d", [DFF, D], BF16).ap()
    wuq_d = nc.dram_tensor("wuq_d", [DFF, D], BF16).ap()
    wdq_d = nc.dram_tensor("wdq_d", [D, DFF], BF16).ap()
    wdqT_d = nc.dram_tensor("wdqT_d", [8, 128, 64, 256], BF16).ap()

    OPS = []
    CNT = defaultdict(int)

    def op(eng, fn, waits=(), inc=None):
        ws = [(s, v) for (s, v) in waits if v > 0]
        if inc is not None:
            CNT[inc[0]] += inc[1]
        OPS.append((eng, tuple(ws), fn, inc))
        return CNT[inc[0]] if inc is not None else None

    def now(sem):
        return CNT[sem]

    import contextlib
    ctx = contextlib.ExitStack()
    with ctx:
        S = {n: ctx.enter_context(nc.semaphore(n))
             for n in ["ldx", "ldg", "lds", "stx", "ve", "ac", "pe", "po"]}

        def sb(name, p, f, dt=BF16):
            return ctx.enter_context(nc.sbuf_tensor(name, [p, f], dt))

        BIGA = sb("BIGA", 128, 32768)
        BIGB = sb("BIGB", 128, 33024)
        BIGC = sb("BIGC", 128, 16384)
        BIGD = sb("BIGD", 128, 16384)
        FA = sb("FA", 128, 3072, F32)
        ident = sb("ident", 128, 128, BF16)
        msk_sb = sb("msk_sb", 128, 288, F32)
        ones_col = sb("ones_col", 128, 1, BF16)
        ones_row = sb("ones_row", 1, 128, BF16)
        onesf_col = sb("onesf_col", 128, 1, F32)
        onesf_row = sb("onesf_row", 1, 128, F32)
        sx_sb = sb("sx_sb", 128, TPC, F32)
        accw = sb("accw", 128, 3, F32)
        tot_sb = sb("tot_sb", 1, 3, F32)
        invs_sb = sb("invs_sb", 1, 3, F32)
        invs_b = sb("invs_b", 128, 3, F32)
        sw_b = sb("sw_b", 128, 3, F32)
        hmax = sb("hmax", 128, TPC, F32)
        dsc_sb = sb("dsc_sb", 128, TPC, F32)
        sxg_sb = sb("sxg_sb", 128, TPC, F32)
        su_sb = sb("su_sb", 128, TPC, F32)
        qsc_sb = sb("qsc_sb", 128, TPC, F32)
        mc_sb = sb("mc_sb", 128, TPC, F32)
        bn6 = sb("bn6", 128, 24, F32)
        mv2 = sb("mv2", 128, 2, F32)
        vv = [sb(f"vv{i}", 128, 1, F32) for i in range(2)]
        sd = [sb(f"sd{i}", 128, 1, F32) for i in range(2)]
        rstd = [sb(f"rstd{i}", 128, 1, F32) for i in range(2)]
        nmr = [sb(f"nmr{i}", 128, 1, F32) for i in range(2)]
        am1 = sb("am1", 128, 1, F32)
        zv = sb("zv", 128, 1, F32)
        zr = sb("zr", 128, 1, F32)

        XO = [FA.ap()[:, 0:2048],
              BIGB.ap()[:, 4096:8192].bitcast(F32)]
        XN32 = BIGB.ap()[:, 0:4096].bitcast(F32)
        T1S = BIGA.ap()[:, 26624:30720].bitcast(F32)
        WCH = [FA.ap()[:, 0:1024], FA.ap()[:, 1024:2048]]
        WCH2 = FA.ap()[:, 2048:3072]
        H4 = [FA.ap()[:, 0:2048],
              BIGB.ap()[:, 23040:27136].bitcast(F32)]
        gate = BIGB.ap()[:, 20992:22016].bitcast(F32)
        ev1 = [BIGB.ap()[:, 16896 + i * 1024: 16896 + (i + 1) * 1024].bitcast(F32)
               for i in range(2)]
        ev2 = [BIGB.ap()[:, 18944 + i * 1024: 18944 + (i + 1) * 1024].bitcast(F32)
               for i in range(2)]
        evf1 = [FA.ap()[:, 512 * i: 512 * (i + 1)] for i in range(2)]
        hstg = [BIGB.ap()[:, 22016 + 512 * i: 22016 + 512 * (i + 1)]
                for i in range(2)]
        evf2 = [FA.ap()[:, 1024 + 512 * i: 1024 + 512 * (i + 1)] for i in range(2)]

        xnfo = BIGA.ap()[:, 0:16384]
        xnbf = [BIGA.ap()[:, 16384 + i * 2048: 16384 + (i + 1) * 2048]
                for i in range(2)]
        xnT = [BIGA.ap()[:, 20480 + i * 2048: 20480 + (i + 1) * 2048]
               for i in range(2)]
        xqbf = BIGA.ap()[:, 24576:26624]

        KQ = BIGA.ap()
        VE_ = BIGB.ap()
        xnwin = BIGC.ap()[:, 0:4096]
        qpbuf = BIGC.ap()[:, 4096:4096 + 4096]
        futs_sb = BIGC.ap()[:, 8192:12288]
        mskb = BIGC.ap()[:, 14592:14848].bitcast(F32)
        wqch = BIGC.ap()[:, 14848:15872]
        tri = BIGC.ap()[:, 14336:14592].bitcast(F32)
        scm = BIGC.ap()[:, 12288:12800]
        pt = [BIGC.ap()[:, 12800:13312], BIGC.ap()[:, 13312:13824]]
        aon = BIGC.ap()[:, 13824:14336]
        wkq = BIGD.ap()[:, 0:8192]
        wvq = BIGD.ap()[:, 8192:16384]

        PS = [ctx.enter_context(nc.psum_tensor(f"PS{i}", [128, 1024], F32))
              for i in range(4)]
        PSbf = [p.ap().bitcast(BF16) for p in PS]

        blk = ctx.enter_context(nc.Block())

        op("gp", lambda g: g.memset(ident.ap(), 0.0))
        op("gp", lambda g: g.affine_select(
            out=ident.ap(), in_=ident.ap(), compare_op=ALU.not_equal, fill=1.0,
            base=0, pattern=[[-1, 128]], channel_multiplier=1))
        op("gp", lambda g: g.memset(ones_col.ap(), 1.0))
        op("gp", lambda g: g.memset(ones_row.ap(), 1.0))
        op("gp", lambda g: g.memset(onesf_col.ap(), 1.0))
        op("gp", lambda g: g.memset(onesf_row.ap(), 1.0))
        op("gp", lambda g: g.memset(accw.ap(), 0.0))
        op("gp", lambda g: g.memset(hmax.ap(), 0.0))
        op("gp", lambda g: g.dma_start(out=msk_sb[:], in_=msk), inc=("ldg", 16))

        def ln_stats(par, src, waits):
            for gi in range(4):
                op("ve", (lambda g0, s0: (lambda v: v.bn_stats(
                    bn6[:, g0 * 6:(g0 + 1) * 6],
                    s0[:, g0 * 512:(g0 + 1) * 512])))(gi, src),
                    waits=waits if gi == 0 else (),
                    inc=("ve", 1) if gi == 3 else None)
            e = now("ve")
            op("ve", lambda v: v.bn_aggr(mv2[:], bn6[:]), waits=[("ve", e)],
               inc=("ve", 1))
            e = now("ve")
            op("ve", (lambda pp: (lambda v: v.tensor_scalar(
                vv[pp][:], mv2[:, 1:2], EPS_LN, None, ALU.add)))(par),
               waits=[("ve", e)], inc=("ve", 1))
            e = now("ve")
            op("ac", (lambda pp: (lambda s: s.activation(
                sd[pp][:], vv[pp][:], AF.Sqrt)))(par),
               waits=[("ve", e)], inc=("ac", 1))
            e = now("ac")
            op("ve", (lambda pp: (lambda v: v.reciprocal(
                rstd[pp][:], sd[pp][:])))(par),
               waits=[("ac", e)], inc=("ve", 1))
            e = now("ve")
            op("ve", (lambda pp: (lambda v: v.scalar_tensor_tensor(
                nmr[pp][:], mv2[:, 0:1], -1.0, rstd[pp][:],
                ALU.mult, ALU.mult)))(par),
               waits=[("ve", e)], inc=("ve", 1))
            return now("ve")

        ac_guard = 0
        for k in range(TPC):
            par = k % 2
            e_ld = op("sy", (lambda kk, pp: (lambda s: s.dma_start(
                out=XO[pp], in_=x_own[kk * 128:(kk + 1) * 128, :])))(k, par),
                waits=[("ve", ac_guard)], inc=("ldx", 16))
            e_st = ln_stats(par, XO[par], [("ldx", e_ld)])
            op("ve", (lambda pp: (lambda v: v.tensor_scalar(
                XN32, XO[pp], rstd[pp][:, 0:1], nmr[pp][:, 0:1],
                ALU.mult, ALU.add)))(par),
                waits=[("ve", e_st)], inc=("ve", 1))
            op("ve", (lambda pp: (lambda v: v.tensor_scalar(
                xnbf[pp], XO[pp], rstd[pp][:, 0:1], nmr[pp][:, 0:1],
                ALU.mult, ALU.add)))(par),
                waits=[("pe", now("pe"))], inc=("ve", 1))
            ac_guard = now("ve")
            e_n = now("ve")
            op("ve", lambda v: v.reduce_max(am1[:], XN32, axis=AX.X,
                                            apply_absolute_value=True),
               waits=[("ve", e_n)], inc=("ve", 1))
            e = now("ve")
            op("ve", lambda v: v.tensor_scalar(am1[:], am1[:], EPS_Q, None,
                                               ALU.max),
               waits=[("ve", e)], inc=("ve", 1))
            e = now("ve")
            op("ve", (lambda kk: (lambda v: v.tensor_scalar(
                sx_sb[:, kk:kk + 1], am1[:], 1.0 / 127.0, None, ALU.mult)))(k),
               waits=[("ve", e)], inc=("ve", 1))
            e = now("ve")
            op("ve", lambda v: v.reciprocal(zr[:], am1[:]),
               waits=[("ve", e)], inc=("ve", 1))
            e = now("ve")
            op("ve", lambda v: v.tensor_scalar(zv[:], zr[:], 127.0, None,
                                               ALU.mult),
               waits=[("ve", e)], inc=("ve", 1))
            e = now("ve")
            op("ve", lambda v: v.tensor_scalar(T1S, XN32, zv[:, 0:1],
                                               MAGIC, ALU.mult, ALU.add),
               waits=[("ve", e)], inc=("ve", 1))
            e = now("ve")
            op("ve", lambda v: v.tensor_scalar(xqbf, T1S, MAGIC, None,
                                               ALU.subtract),
               waits=[("ve", e)], inc=("ve", 1))
            e_xq = now("ve")
            for which, (srcap, dstap) in enumerate(
                    ((xnbf[par], xnfo), (xqbf, BIGC.ap()))):
                egate_ve = e_n if which == 0 else e_xq
                for ic in range(16):
                    pp = ic % 2
                    e_pe = op("pe", (lambda s0, icc, ppp: (lambda t: t.transpose(
                        PSbf[ppp][:, 0:128], s0[:, icc * 128:(icc + 1) * 128],
                        ident[:])))(srcap, ic, pp),
                        waits=[("ve", max(egate_ve, now("ve")))],
                        inc=("pe", 1))
                    op("ve", (lambda d0, icc, kk, ppp: (lambda v: v.tensor_copy(
                        d0[:, icc * 1024 + kk * 128: icc * 1024 + (kk + 1) * 128],
                        PSbf[ppp][:, 0:128])))(dstap, ic, k, pp),
                        waits=[("pe", e_pe)], inc=("ve", 1))
            e_v = now("ve")
            op("sy", (lambda kk: (lambda s: s.dma_start(
                out=xq_d.rearrange("a p n -> p a n")[
                    :, :, kk * 128:(kk + 1) * 128],
                in_=BIGC.ap().rearrange("p (a n) -> p a n", a=16)[
                    :, :, kk * 128:(kk + 1) * 128])))(k),
                waits=[("ve", e_v)], inc=("stx", 16))
        E_A_spill = now("stx")

        e_prev_dma = [0, 0]
        for tt in range(NT):
            par = tt % 2
            e_ld = op("sy", (lambda t0, pp: (lambda s: s.dma_start(
                out=XO[pp], in_=x_full[t0 * 128:(t0 + 1) * 128, :])))(tt, par),
                waits=[("ve", now("ve"))], inc=("ldx", 16))
            e_st = ln_stats(par, XO[par], [("ldx", e_ld)])
            op("ve", (lambda pp: (lambda v: v.tensor_scalar(
                xnbf[pp], XO[pp], rstd[pp][:, 0:1], nmr[pp][:, 0:1],
                ALU.mult, ALU.add)))(par),
                waits=[("ve", e_st), ("pe", now("pe"))], inc=("ve", 1))
            e_n = now("ve")
            for ic in range(16):
                pp = ic % 2
                e_pe = op("pe", (lambda ppx, icc, ppp: (lambda t: t.transpose(
                    PSbf[ppp][:, 0:128], xnbf[ppx][:, icc * 128:(icc + 1) * 128],
                    ident[:])))(par, ic, pp),
                    waits=[("ve", e_n), ("stx", e_prev_dma[par]),
                           ("ve", now("ve"))],
                    inc=("pe", 1))
                op("ve", (lambda ppx, icc, ppp: (lambda v: v.tensor_copy(
                    xnT[ppx][:, icc * 128:(icc + 1) * 128],
                    PSbf[ppp][:, 0:128])))(par, ic, pp),
                    waits=[("pe", e_pe)], inc=("ve", 1))
            e_v = now("ve")
            e_prev_dma[par] = op("sy", (lambda t0, pp: (lambda s: s.dma_start(
                out=xnf_d[t0].rearrange("a p n -> p a n"),
                in_=xnT[pp].rearrange("p (a n) -> p a n", a=16))))(tt, par),
                waits=[("ve", e_v)], inc=("stx", 16))
        E_B = now("stx")

        for oc in range(16):
            par = oc % 2
            wslc = BIGD.ap()[:, par * 8192: par * 8192 + 2048]
            e_w = op("gp", (lambda occ, hf: (lambda g: g.dma_start(
                out=hf.rearrange("p (a n) -> p a n", a=16),
                in_=Wq.rearrange("(a p) n -> p a n", p=128)[
                    :, :, occ * 128:(occ + 1) * 128])))(oc, wslc),
                waits=[("pe", now("pe"))],
                inc=("ldg", 16))
            for th in range(2):
                pp = th
                e_mm = None
                for ic in range(16):
                    e_mm = op("pe", (lambda hf, icc, thh, ppp: (lambda t: t.matmul(
                        PS[2 + ppp][:, 0:512],
                        hf[:, icc * 128:(icc + 1) * 128],
                        xnfo[:, icc * 1024 + thh * 512: icc * 1024 + thh * 512 + 512],
                        start=(icc == 0), stop=(icc == 15))))(wslc, ic, th, pp),
                        waits=([("ldg", e_w), ("ve", now("ve"))]
                               if ic == 0 else ()),
                        inc=("pe", 1) if ic == 15 else None)
                op("ve", (lambda ppp: (lambda v: v.tensor_copy(
                    pt[ppp], PS[2 + ppp][:, 0:512])))(pp),
                    waits=[("pe", e_mm), ("stx", max(E_A_spill, now("stx") - 32))],
                    inc=("ve", 1))
                e_v = now("ve")
                op("sy", (lambda occ, thh, ppp: (lambda s: s.dma_start(
                    out=q_d[occ, :, thh * 512: thh * 512 + 512],
                    in_=pt[ppp])))(oc, th, pp),
                    waits=[("ve", e_v)], inc=("stx", 16))
        E_C = now("stx")

        wlist = [(wg, 0, 64), (wu, 1, 64), (wd, 2, 16)]
        wchunks = []
        for (w_ap, widx, ntile) in wlist:
            ncol = w_ap.shape[1] // 1024
            for t0 in range(ntile):
                for cc in range(ncol):
                    wchunks.append((w_ap, widx, t0, cc))
        assert len(wchunks) == 384

        wch_i = [0]

        def emit_scale_chunks(n):
            for _ in range(n):
                i = wch_i[0]
                if i >= len(wchunks):
                    return
                wch_i[0] += 1
                w_ap, widx, t0, cc = wchunks[i]
                par = i % 2
                e_d = op("sc", (lambda wa, tt0, ccc, pp: (lambda g: g.dma_start(
                    out=WCH[pp],
                    in_=wa[tt0 * 128:(tt0 + 1) * 128,
                           ccc * 1024:(ccc + 1) * 1024])))(w_ap, t0, cc, par),
                    waits=[("ve", now("ve"))], inc=("lds", 16))
                op("ve", (lambda pp: (lambda v: v.reduce_sum(
                    am1[:], WCH[pp], axis=AX.X,
                    apply_absolute_value=True)))(par),
                    waits=[("lds", e_d)], inc=("ve", 1))
                e = now("ve")
                op("ve", (lambda wi: (lambda v: v.tensor_add(
                    accw[:, wi:wi + 1], accw[:, wi:wi + 1], am1[:])))(widx),
                    waits=[("ve", e)], inc=("ve", 1))

        def emit_scale_final():
            for wi in range(3):
                e_pe = op("pe", (lambda w2: (lambda t: t.matmul(
                    PS[3][0:1, 512 + w2:512 + w2 + 1], accw[:, w2:w2 + 1],
                    onesf_col[:], start=True, stop=True)))(wi),
                    waits=[("ve", now("ve")), ("ac", now("ac"))],
                    inc=("pe", 1))
                op("ac", (lambda w2: (lambda s: s.copy(
                    tot_sb[0:1, w2:w2 + 1],
                    PS[3][0:1, 512 + w2:512 + w2 + 1])))(wi),
                    waits=[("pe", e_pe)], inc=("ac", 1))
            e = now("ac")
            op("ve", lambda v: v.tensor_scalar(tot_sb[:], tot_sb[:],
                                               1.0 / NW_EL, EPS_Q,
                                               ALU.mult, ALU.add),
               waits=[("ac", e)], inc=("ve", 1))
            e = now("ve")
            op("ve", lambda v: v.reciprocal(invs_sb[:], tot_sb[:]),
               waits=[("ve", e)], inc=("ve", 1))
            e_r = now("ve")
            for wi in range(3):
                for (src, dst) in ((invs_sb, invs_b), (tot_sb, sw_b)):
                    e_pe = op("pe", (lambda s0, w2: (lambda t: t.matmul(
                        PS[3][:, 512 + w2:512 + w2 + 1], onesf_row[:],
                        s0[0:1, w2:w2 + 1], start=True, stop=True)))(src, wi),
                        waits=[("ve", e_r), ("ve", now("ve"))],
                        inc=("pe", 1))
                    op("ve", (lambda d0, w2: (lambda v: v.tensor_copy(
                        d0[:, w2:w2 + 1],
                        PS[3][:, 512 + w2:512 + w2 + 1])))(dst, wi),
                        waits=[("pe", e_pe)], inc=("ve", 1))

        wq_i = [0]
        wd_st = [0]
        wq_dsts = {0: wgq_d, 1: wuq_d, 2: wdq_d}

        def emit_quant_chunks(n):
            for _ in range(n):
                i = wq_i[0]
                if i >= len(wchunks):
                    return
                wq_i[0] += 1
                w_ap, widx, t0, cc = wchunks[i]
                par = i % 2
                e_d = op("sc", (lambda wa, tt0, ccc, pp: (lambda g: g.dma_start(
                    out=WCH[pp],
                    in_=wa[tt0 * 128:(tt0 + 1) * 128,
                           ccc * 1024:(ccc + 1) * 1024])))(w_ap, t0, cc, par),
                    waits=[("ve", now("ve"))], inc=("lds", 16))
                op("ve", (lambda wi2, pp: (lambda v: v.tensor_scalar(
                    WCH2, WCH[pp], invs_b[:, wi2:wi2 + 1], 1.0,
                    ALU.mult, ALU.min)))(widx, par),
                    waits=[("lds", e_d), ("lds", wd_st[0])], inc=("ve", 1))
                op("ve", (lambda pp: (lambda v: v.tensor_scalar(
                    WCH[pp], WCH2, -1.0, MAGIC, ALU.max, ALU.add)))(par),
                    inc=("ve", 1))
                op("ve", (lambda pp: (lambda v: v.tensor_scalar(
                    wqch, WCH[pp], MAGIC, None, ALU.subtract)))(par),
                    waits=[("lds", now("lds"))], inc=("ve", 1))
                e = now("ve")
                if widx < 2:
                    op("sc", (lambda wi2, tt0, ccc: (lambda g: g.dma_start(
                        out=wq_dsts[wi2][tt0 * 128:(tt0 + 1) * 128,
                                         ccc * 1024:(ccc + 1) * 1024],
                        in_=wqch)))(widx, t0, cc),
                        waits=[("ve", e)], inc=("lds", 16))
                else:
                    stgbf = WCH2.bitcast(BF16)
                    for b in range(8):
                        ppb = b % 2
                        e_tp = op("pe", (lambda bb, pb: (lambda t: t.transpose(
                            PSbf[3][:, pb * 1024: pb * 1024 + 128],
                            wqch[:, bb * 128:(bb + 1) * 128],
                            ident[:])))(b, ppb),
                            waits=[("ve", now("ve"))], inc=("pe", 1))
                        op("ve", (lambda bb, pb: (lambda v: v.tensor_copy(
                            stgbf[:, bb * 128:(bb + 1) * 128],
                            PSbf[3][:, pb * 1024: pb * 1024 + 128])))(b, ppb),
                            waits=[("pe", e_tp)], inc=("ve", 1))
                    e2 = now("ve")
                    o8, obb = t0 // 2, t0 % 2
                    wd_st[0] = op("sc", (lambda oo8, oob, ccc2: (lambda g: g.dma_start(
                        out=wdqT_d.rearrange("a p c n -> p a c n")[
                            :, oo8, ccc2 * 8:(ccc2 + 1) * 8,
                            oob * 128: oob * 128 + 128],
                        in_=stgbf[:, 0:1024].rearrange(
                            "p (b f) -> p b f", b=8))))(o8, obb, cc),
                        waits=[("ve", e2)], inc=("lds", 16))

        RHS0 = [0, 129, 258, 387]
        AOOF = [0, 129, 512, 641]
        ZREL = [0, 128, 0, 128]
        AOREL = [1, 0, 1, 0]

        op("gp", lambda g: g.iota(tri, pattern=[[-1, 128]], base=0,
                                  channel_multiplier=1,
                                  allow_small_or_imprecise_dtypes=True),
           waits=[("stx", E_A_spill)], inc=("po", 1))

        e_aon_dma = 0
        for qp in range(4):
            e_wk = op("gp", (lambda qq: (lambda g: g.dma_start(
                out=wkq.rearrange("p (a n) -> p a n", a=16),
                in_=Wk.rearrange("(a p) n -> p a n", p=128)[
                    :, :, qq * 512:(qq + 1) * 512])))(qp),
                waits=[("pe", now("pe"))], inc=("ldg", 16))
            e_wv = op("gp", (lambda qq: (lambda g: g.dma_start(
                out=wvq.rearrange("p (a n) -> p a n", a=16),
                in_=Wv.rearrange("(a p) n -> p a n", p=128)[
                    :, :, qq * 512:(qq + 1) * 512])))(qp),
                inc=("ldg", 16))
            op("gp", lambda g: g.memset(
                VE_.rearrange("p (c b) -> p c b", b=516)[:, :, 0:1], 1.0),
                waits=[("pe", now("pe"))])
            op("gp", lambda g: g.memset(
                VE_.rearrange("p (c b) -> p c b", b=516)[:, :, 257:259], 1.0))
            op("gp", lambda g: g.memset(
                VE_.rearrange("p (c b) -> p c b", b=516)[:, :, 515:516], 1.0),
                inc=("po", 1))
            e_ones = now("po")

            LASTW = {}
            E_COPY = {}
            for tw in range(32):
                wbuf = xnwin if tw % 2 == 0 else qpbuf
                e_x = None
                for s2 in range(2):
                    e_x = op("sy", (lambda tww, ss2, wb2: (lambda s: s.dma_start(
                        out=wb2.rearrange("p (a n2) -> p a n2", n2=256)[
                            :, :, ss2 * 128:(ss2 + 1) * 128],
                        in_=xnf_d[2 * tww + ss2].rearrange("a p n -> p a n"))))(
                            tw, s2, wbuf),
                        waits=(([("stx", E_B), ("pe", now("pe"))] if tw < 2
                                else [("stx", E_B), ("pe", LASTW[tw - 2])])
                               if s2 == 0 else ()),
                        inc=("ldx", 16))
                last_k = None
                last_v = None
                for ic in range(16):
                    for oc4 in range(4):
                        last_k = op("pe", (lambda icc, o4, wb2: (lambda t: t.matmul(
                            PS[0][:, o4 * 256:(o4 + 1) * 256],
                            wkq[:, icc * 512 + o4 * 128:
                                icc * 512 + o4 * 128 + 128],
                            wb2[:, icc * 256: icc * 256 + 256],
                            start=(icc == 0 and o4 % 2 == 0),
                            stop=(icc == 15))))(ic, oc4, wbuf),
                            waits=([("ldx", e_x), ("ldg", e_wv),
                                    ("ve", now("ve") if tw == 0
                                     else E_COPY[tw - 1])]
                                   if (ic == 0 and oc4 == 0) else ()),
                            inc=("pe", 1) if (ic == 15 and oc4 == 3) else None)
                    for st2 in range(2):
                        last_v = op("pe", (lambda icc, s2, wb2: (lambda t: t.matmul(
                            PS[1][:, s2 * 512:(s2 + 1) * 512],
                            wb2[:, icc * 256 + s2 * 128:
                                icc * 256 + s2 * 128 + 128],
                            wvq[:, icc * 512: icc * 512 + 512],
                            start=(icc == 0), stop=(icc == 15))))(ic, st2, wbuf),
                            inc=("pe", 1) if (ic == 15 and st2 == 1) else None)
                LASTW[tw] = now("pe")
                for oc4 in range(4):
                    op("ve", (lambda tww, o4: (lambda v: v.tensor_copy(
                        KQ[:, o4 * T + tww * 256: o4 * T + tww * 256 + 256],
                        PS[0][:, o4 * 256:(o4 + 1) * 256])))(tw, oc4),
                        waits=[("pe", last_k), ("po", e_ones)]
                        if oc4 == 0 else (),
                        inc=("ve", 1))
                for st2 in range(2):
                    ch = tw * 2 + st2
                    for half in range(2):
                        op("ve", (lambda c0, s2, hh: (lambda v: v.tensor_copy(
                            VE_[:, c0 * 516 + (1 if hh == 0 else 259):
                                c0 * 516 + (1 if hh == 0 else 259) + 256],
                            PS[1][:, s2 * 512 + hh * 256:
                                  s2 * 512 + hh * 256 + 256])))(ch, st2, half),
                            waits=[("pe", last_v)]
                            if (st2 == 0 and half == 0) else (),
                            inc=("ve", 1))
                E_COPY[tw] = now("ve")
                if qp < 2:
                    emit_scale_chunks(6)
                else:
                    emit_quant_chunks(6)
            E_kv = now("ve")

            op("gp", lambda g: g.memset(futs_sb[:, 7 * 512:8 * 512], 0.0),
               waits=[("ac", now("ac"))], inc=("po", 1))
            e_snap = now("ac")
            first = True
            e_fm = None
            for j in range(NT - 1, 7, -1):
                for half in range(2):
                    e_fm = op("pe", (lambda jj, hh, ff: (lambda t: t.matmul(
                        PS[3][0:1, hh * 256: hh * 256 + 256],
                        ones_col[:],
                        VE_[:, jj * 516 + (1 if hh == 0 else 259):
                            jj * 516 + (1 if hh == 0 else 259) + 256],
                        start=ff, stop=False)))(j, half, first and half == 0),
                        waits=([("ve", E_kv), ("ac", e_snap)]
                               if half == 0 else ()),
                        inc=("pe", 1) if half == 1 else None)
                first = False
                if j % 8 == 0:
                    kcls = j // 8 - 1
                    op("ac", (lambda kc: (lambda s: s.copy(
                        futs_sb[0:1, kc * 512:(kc + 1) * 512],
                        PS[3][0:1, 0:512])))(kcls),
                        waits=[("pe", e_fm)], inc=("ac", 1))
                    e_snap = now("ac")
            E_futs = now("ac")

            e_q = op("sy", (lambda qq: (lambda s: s.dma_start(
                out=qpbuf.rearrange("p (a n) -> p a n", a=4),
                in_=q_d[qq * 4:(qq + 1) * 4].rearrange("a p n -> p a n"))))(qp),
                waits=[("stx", E_C), ("pe", now("pe")), ("po", e_ones)],
                inc=("ldx", 16))

            slot_base = 0
            for k8 in range(TPC):
                nslots = 8 * k8 + 8
                aops = PS[k8 % 2]
                e_ao_free = now("ve")
                last_pv = None
                for j in range(nslots):
                    spar = j % 2
                    e_qk = None
                    for hh in range(4):
                        wts = ()
                        if hh == 0:
                            if j == 0:
                                wts = [("ldx", e_q), ("ac", now("ac")),
                                       ("ve", e_ao_free)]
                            else:
                                wts = [("ac", max(0, now("ac") - 1)),
                                       ("ve", now("ve"))]
                        e_qk = op("pe", (lambda jj, h2, kk8, sp: (lambda t: t.matmul(
                            PS[2][:, sp * 512 + h2 * 128:
                                  sp * 512 + h2 * 128 + 128],
                            KQ[:, h2 * T + jj * 128: h2 * T + jj * 128 + 128],
                            qpbuf[:, h2 * 1024 + kk8 * 128:
                                  h2 * 1024 + kk8 * 128 + 128],
                            start=True, stop=True)))(j, hh, k8, spar),
                            waits=wts,
                            inc=("pe", 1) if hh == 3 else None)
                    masked = (j >= nslots - 8)
                    if masked:
                        slot = slot_base + j
                        op("ve", (lambda sl: (lambda v: v.tensor_scalar(
                            mskb, tri, msk_sb[:, sl:sl + 1], None,
                            ALU.is_le)))(slot),
                            waits=[("pe", e_qk)], inc=("ve", 1))
                        e_m = None
                        for hh in range(4):
                            e_m = op("ve", (lambda h2, sp: (lambda v: v.tensor_mul(
                                scm[:, h2 * 128:(h2 + 1) * 128],
                                PS[2][:, sp * 512 + h2 * 128:
                                      sp * 512 + h2 * 128 + 128],
                                mskb)))(hh, spar),
                                inc=("ve", 1))
                        e_exp = op("ac", (lambda sp: (lambda s: s.activation(
                            pt[sp], scm, AF.Exp, scale=INV_TAU)))(spar),
                            waits=[("ve", e_m)], inc=("ac", 1))
                    else:
                        e_exp = op("ac", (lambda sp: (lambda s: s.activation(
                            pt[sp], PS[2][:, sp * 512: sp * 512 + 512],
                            AF.Exp, scale=INV_TAU)))(spar),
                            waits=[("pe", e_qk)], inc=("ac", 1))
                    for hh in range(4):
                        last_pv = op("pe", (lambda jj, h2, sp, k9: (lambda t: t.matmul(
                            k9[:, AOOF[h2]: AOOF[h2] + 129],
                            pt[sp][:, h2 * 128:(h2 + 1) * 128],
                            VE_[:, jj * 516 + RHS0[h2]:
                                jj * 516 + RHS0[h2] + 129],
                            start=(jj == 0 and h2 % 2 == 0),
                            stop=False)))(j, hh, spar, aops),
                            waits=[("ac", e_exp)] if hh == 0 else (),
                            inc=("pe", 1) if hh == 3 else None)
                if k8 < 7:
                    for hh in range(4):
                        last_pv = op("pe", (lambda h2, kk8, k9: (lambda t: t.matmul(
                            k9[:, AOOF[h2] + AOREL[h2]:
                               AOOF[h2] + AOREL[h2] + 128],
                            ones_row[:],
                            futs_sb[0:1, kk8 * 512 + h2 * 128:
                                    kk8 * 512 + h2 * 128 + 128],
                            start=False, stop=(h2 == 3))))(hh, k8, aops),
                            waits=[("ac", E_futs)] if hh == 0 else (),
                            inc=("pe", 1) if hh == 3 else None)
                futcnt = float(128 * (NT - nslots))
                for hh in range(4):
                    e1 = op("ve", (lambda h2, fc, k9: (lambda v: v.tensor_scalar(
                        zv[:], k9[:, AOOF[h2] + ZREL[h2]:
                                  AOOF[h2] + ZREL[h2] + 1], fc, None,
                        ALU.add)))(hh, futcnt, aops),
                        waits=([("pe", last_pv), ("stx", e_aon_dma)]
                               if hh == 0 else [("ve", now("ve"))]),
                        inc=("ve", 1))
                    op("ve", lambda v: v.reciprocal(zr[:], zv[:]),
                       waits=[("ve", e1)], inc=("ve", 1))
                    e2 = now("ve")
                    op("ve", (lambda h2, k9: (lambda v: v.tensor_scalar(
                        aon[:, h2 * 128:(h2 + 1) * 128],
                        k9[:, AOOF[h2] + AOREL[h2]:
                           AOOF[h2] + AOREL[h2] + 128],
                        zr[:, 0:1], None, ALU.mult)))(hh, aops),
                        waits=[("ve", e2)], inc=("ve", 1))
                e_v = now("ve")
                e_aon_dma = op("sy", (lambda kk8, qq: (lambda s: s.dma_start(
                    out=ao_d[kk8 * 128:(kk8 + 1) * 128,
                             qq * 512:(qq + 1) * 512],
                    in_=aon)))(k8, qp),
                    waits=[("ve", e_v)], inc=("stx", 16))
                slot_base += nslots

            if qp == 0:
                emit_scale_chunks(0)
            elif qp == 1:
                emit_scale_chunks(192)
                emit_scale_final()
            elif qp == 2:
                emit_quant_chunks(0)
            else:
                emit_quant_chunks(192)
        emit_quant_chunks(384)
        E_D = now("stx")
        E_wq = now("lds")

        e_xq = op("sy", lambda s: s.dma_start(
            out=BIGC.ap().rearrange("p (a n) -> p a n", a=16),
            in_=xq_d.rearrange("a p n -> p a n")),
            waits=[("stx", E_A_spill), ("pe", now("pe")),
                   ("ve", now("ve"))],
            inc=("ldx", 16))
        xqf = BIGC.ap()
        e_pre = now("ve")
        for k in range(TPC):
            op("ve", (lambda kk: (lambda v: v.tensor_scalar(
                sxg_sb[:, kk:kk + 1], sx_sb[:, kk:kk + 1], sw_b[:, 0:1],
                None, ALU.mult)))(k),
                waits=[("ve", e_pre)] if k == 0 else (), inc=("ve", 1))
            op("ve", (lambda kk: (lambda v: v.tensor_scalar(
                su_sb[:, kk:kk + 1], sx_sb[:, kk:kk + 1], sw_b[:, 1:2],
                None, ALU.mult)))(k), inc=("ve", 1))
        E_sc = now("ve")

        e_h_dma = {}
        E1LAST = {}
        for opan in range(32):
            par = opan % 2
            wgT = BIGD.ap()[:, par * 8192: par * 8192 + 4096]
            wuT = BIGD.ap()[:, par * 8192 + 4096: par * 8192 + 8192]
            e_g = None
            e_u = None
            for ic in range(16):
                e_g = op("sy", (lambda icc, opp, wb: (lambda s: s.dma_start(
                    out=wb[:, icc * 256: icc * 256 + 256],
                    in_=wgq_d[opp * 256:(opp + 1) * 256,
                              icc * 128:(icc + 1) * 128],
                    transpose=True)))(ic, opan, wgT),
                    waits=(() if ic != 0 else
                           ([("lds", E_wq), ("pe", now("pe")),
                             ("ldx", e_xq)] if opan < 2 else
                            [("lds", E_wq), ("pe", E1LAST[opan - 2])])),
                    inc=("ldx", 16))
                e_u = op("sy", (lambda icc, opp, wb: (lambda s: s.dma_start(
                    out=wb[:, icc * 256: icc * 256 + 256],
                    in_=wuq_d[opp * 256:(opp + 1) * 256,
                              icc * 128:(icc + 1) * 128],
                    transpose=True)))(ic, opan, wuT),
                    waits=(() if ic != 0 else
                           ([("lds", E_wq), ("pe", now("pe")),
                             ("ac", now("ac"))] if opan < 2 else ())),
                    inc=("ldx", 16))
            for tk in range(TPC):
                ppar = tk % 2
                e_mm = None
                for ic in range(16):
                    op("pe", (lambda icc, tkk, wb, pq: (lambda t: t.matmul(
                        PS[pq][:, 0:256],
                        xqf[:, icc * 1024 + tkk * 128:
                            icc * 1024 + tkk * 128 + 128],
                        wb[:, icc * 256: icc * 256 + 256],
                        start=(icc == 0), stop=(icc == 15))))(ic, tk, wgT, ppar),
                        waits=([("ldx", e_g), ("ldx", e_u),
                                ("ac", now("ac")), ("ve", now("ve"))]
                               if ic == 0 else ()),
                        inc=("pe", 1) if ic == 15 else None)
                    e_mm = op("pe", (lambda icc, tkk, wb, pq: (lambda t: t.matmul(
                        PS[pq][:, 512:768],
                        xqf[:, icc * 1024 + tkk * 128:
                            icc * 1024 + tkk * 128 + 128],
                        wb[:, icc * 256: icc * 256 + 256],
                        start=(icc == 0), stop=(icc == 15))))(ic, tk, wuT, ppar),
                        inc=("pe", 1) if ic == 15 else None)
                e_sig = op("ac", (lambda tkk, pq: (lambda s: s.activation(
                    gate[:, 0:256], PS[pq][:, 0:256], AF.Sigmoid,
                    scale=sxg_sb[:, tkk:tkk + 1])))(tk, ppar),
                    waits=[("pe", e_mm), ("ve", E_sc),
                           ("ve", now("ve"))], inc=("ac", 1))
                hpar = (opan * 8 + tk) % 2
                e_h = op("ve", (lambda pq, hp: (lambda v: v.tensor_mul(
                    hstg[hp][:, 0:256], gate[:, 0:256],
                    PS[pq][:, 512:768])))(ppar, hpar),
                    waits=[("ac", e_sig),
                           ("stx", e_h_dma.get(hpar, 0))],
                    inc=("ve", 1))
                op("ve", (lambda hp: (lambda v: v.reduce_max(
                    am1[:], hstg[hp][:, 0:256], axis=AX.X,
                    apply_absolute_value=True)))(hpar),
                    waits=[("ve", e_h)], inc=("ve", 1))
                e = now("ve")
                op("ve", (lambda tkk: (lambda v: v.tensor_max(
                    hmax[:, tkk:tkk + 1], hmax[:, tkk:tkk + 1], am1[:])))(tk),
                    waits=[("ve", e)], inc=("ve", 1))
                e_v = now("ve")
                e_h_dma[hpar] = op("sy", (lambda tkk, opp, hp: (lambda s: s.dma_start(
                    out=h_d[tkk * 128:(tkk + 1) * 128,
                            opp * 256:(opp + 1) * 256],
                    in_=hstg[hp][:, 0:256])))(tk, opan, hpar),
                    waits=[("ve", e_v)], inc=("stx", 16))
            E1LAST[opan] = now("pe")
        E_E1 = now("stx")

        e0 = now("ve")
        for tk in range(TPC):
            op("ve", (lambda kk: (lambda v: v.tensor_scalar(
                zv[:], hmax[:, kk:kk + 1], su_sb[:, kk:kk + 1], None,
                ALU.mult)))(tk),
                waits=[("ve", e0), ("stx", E_E1)] if tk == 0
                else [("ve", now("ve"))],
                inc=("ve", 1))
            e = now("ve")
            op("ve", (lambda kk: (lambda v: v.tensor_scalar(
                mc_sb[:, kk:kk + 1], zv[:], EPS_Q, None, ALU.max)))(tk),
                waits=[("ve", e)], inc=("ve", 1))
            e = now("ve")
            op("ve", (lambda kk: (lambda v: v.reciprocal(
                zr[:], mc_sb[:, kk:kk + 1])))(tk),
                waits=[("ve", e)], inc=("ve", 1))
            e = now("ve")
            op("ve", (lambda kk: (lambda v: v.tensor_scalar(
                zv[:], zr[:], su_sb[:, kk:kk + 1], None, ALU.mult)))(tk),
                waits=[("ve", e)], inc=("ve", 1))
            e = now("ve")
            op("ve", (lambda kk: (lambda v: v.tensor_scalar(
                qsc_sb[:, kk:kk + 1], zv[:], 127.0, None, ALU.mult)))(tk),
                waits=[("ve", e)], inc=("ve", 1))
            e = now("ve")
            op("ve", (lambda kk: (lambda v: v.tensor_scalar(
                zv[:], mc_sb[:, kk:kk + 1], 1.0 / 127.0, None,
                ALU.mult)))(tk), waits=[("ve", e)], inc=("ve", 1))
            e = now("ve")
            op("ve", (lambda kk: (lambda v: v.tensor_scalar(
                dsc_sb[:, kk:kk + 1], zv[:], sw_b[:, 2:3], None,
                ALU.mult)))(tk), waits=[("ve", e)], inc=("ve", 1))
        E_qs = now("ve")

        hload = BIGC.ap()[:, 0:8192]
        hT = [BIGB.ap()[:, i * 8192:(i + 1) * 8192] for i in range(2)]
        for grp in range(4):
            tset = [grp * 2, grp * 2 + 1]
            for ti, tk in enumerate(tset):
                e_hl = op("sy", (lambda tkk: (lambda s: s.dma_start(
                    out=hload, in_=h_d[tkk * 128:(tkk + 1) * 128, :])))(tk),
                    waits=[("stx", E_E1), ("pe", now("pe")),
                           ("ve", now("ve"))],
                    inc=("ldx", 16))
                for hf in range(4):
                    op("ve", (lambda tkk, h2: (lambda v: v.tensor_scalar(
                        H4[h2 % 2], hload[:, h2 * 2048: h2 * 2048 + 2048],
                        qsc_sb[:, tkk:tkk + 1], MAGIC, ALU.mult,
                        ALU.add)))(tk, hf),
                        waits=[("ldx", e_hl), ("ve", E_qs)]
                        if hf == 0 else [("ve", now("ve"))],
                        inc=("ve", 1))
                    e = now("ve")
                    op("ve", (lambda h2: (lambda v: v.tensor_scalar(
                        hload[:, h2 * 2048: h2 * 2048 + 2048], H4[h2 % 2],
                        MAGIC, None, ALU.subtract)))(hf),
                        waits=[("ve", e)], inc=("ve", 1))
                e_hq = now("ve")
                for ffc in range(64):
                    pp = ffc % 2
                    e_pe = op("pe", (lambda fc, ppp: (lambda t: t.transpose(
                        PSbf[2 + ppp][:, 0:128],
                        hload[:, fc * 128: fc * 128 + 128],
                        ident[:])))(ffc, pp),
                        waits=[("ve", e_hq if ffc < 2 else now("ve"))],
                        inc=("pe", 1))
                    op("ve", (lambda ti2, fc, ppp: (lambda v: v.tensor_copy(
                        hT[ti2][:, fc * 128: fc * 128 + 128],
                        PSbf[2 + ppp][:, 0:128])))(ti, ffc, pp),
                        waits=[("pe", e_pe)], inc=("ve", 1))
            E_hT = now("ve")
            E2LAST = {}
            for op8 in range(8):
                wdT = BIGA.ap()[:, (op8 % 2) * 16384:(op8 % 2) * 16384 + 16384]
                e_w = op("sy", (lambda o8, wb: (lambda s: s.dma_start(
                    out=wb[:, 0:16384],
                    in_=wdqT_d[o8].rearrange("p c n -> p (c n)"))))(op8, wdT),
                    waits=([("pe", now("pe")), ("ve", E_hT), ("lds", E_wq)]
                           if op8 < 2 else
                           [("pe", E2LAST[op8 - 2]), ("lds", E_wq)]),
                    inc=("ldx", 16))
                for ti, tk in enumerate(tset):
                    ppar = ti % 2
                    e_mm = None
                    for ffc in range(64):
                        e_mm = op("pe", (lambda ti2, fc, pq, wb: (lambda t: t.matmul(
                            PS[pq][:, 0:256],
                            hT[ti2][:, fc * 128: fc * 128 + 128],
                            wb[:, fc * 256: fc * 256 + 256],
                            start=(fc == 0), stop=(fc == 63))))(ti, ffc, ppar, wdT),
                            waits=([("ldx", e_w), ("ve", now("ve"))]
                                   if ffc == 0 else ()),
                            inc=("pe", 1) if ffc == 63 else None)
                    e_x2 = op("sy", (lambda tkk, o8, pq: (lambda s: s.dma_start(
                        out=ev1[pq][:, 0:256],
                        in_=x_own[tkk * 128:(tkk + 1) * 128,
                                  o8 * 256:(o8 + 1) * 256])))(tk, op8, ppar),
                        waits=[("ve", now("ve"))], inc=("ldx", 16))
                    op("ve", (lambda tkk, pq: (lambda v: v.scalar_tensor_tensor(
                        ev2[pq][:, 0:256], PS[pq][:, 0:256],
                        dsc_sb[:, tkk:tkk + 1], ev1[pq][:, 0:256],
                        ALU.mult, ALU.add)))(tk, ppar),
                        waits=[("pe", e_mm), ("ldx", e_x2),
                               ("stx", now("stx") - 32)],
                        inc=("ve", 1))
                    e_v = now("ve")
                    op("sy", (lambda tkk, o8, pq: (lambda s: s.dma_start(
                        out=res_d[tkk * 128:(tkk + 1) * 128,
                                  o8 * 256:(o8 + 1) * 256],
                        in_=ev2[pq][:, 0:256])))(tk, op8, ppar),
                        waits=[("ve", e_v)], inc=("stx", 16))
                E2LAST[op8] = now("pe")
        E_res = now("stx")

        aof = BIGC.ap()
        e_ao = None
        for ic in range(16):
            e_ao = op("sy", (lambda icc: (lambda s: s.dma_start(
                out=aof[:, icc * 1024:(icc + 1) * 1024],
                in_=ao_d[:, icc * 128:(icc + 1) * 128],
                transpose=True)))(ic),
                waits=([("stx", E_D), ("pe", now("pe")),
                        ("ldx", now("ldx"))] if ic == 0 else ()),
                inc=("ldx", 16))
        for woh in range(2):
            wo_sb = BIGB.ap()[:, woh * 16384: woh * 16384 + 16384]
            e_wo = op("gp", (lambda wh, wb: (lambda g: g.dma_start(
                out=wb.rearrange("p (a n) -> p a n", a=16),
                in_=Wo.rearrange("(a p) n -> p a n", p=128)[
                    :, :, wh * 1024:(wh + 1) * 1024])))(woh, wo_sb),
                waits=[("pe", now("pe")), ("ve", now("ve"))],
                inc=("ldg", 16))
            for tk in range(TPC):
                for op2 in range(2):
                    ppar = (tk * 2 + op2) % 2
                    e_mm = None
                    for ic in range(16):
                        e_mm = op("pe", (lambda icc, tkk, o2, pq, wb: (lambda t: t.matmul(
                            PS[pq][:, 0:512],
                            aof[:, icc * 1024 + tkk * 128:
                                icc * 1024 + tkk * 128 + 128],
                            wb[:, icc * 1024 + o2 * 512:
                               icc * 1024 + o2 * 512 + 512],
                            start=(icc == 0), stop=(icc == 15))))(
                                ic, tk, op2, ppar, wo_sb),
                            waits=([("ldx", e_ao), ("ldg", e_wo),
                                    ("ve", now("ve"))] if ic == 0 else ()),
                            inc=("pe", 1) if ic == 15 else None)
                    oc0 = woh * 1024 + op2 * 512
                    e_r = op("sy", (lambda tkk, o0, pq: (lambda s: s.dma_start(
                        out=evf1[pq],
                        in_=res_d[tkk * 128:(tkk + 1) * 128,
                                  o0:o0 + 512])))(tk, oc0, ppar),
                        waits=[("stx", E_res), ("ve", now("ve"))],
                        inc=("ldx", 16))
                    op("ve", (lambda pq: (lambda v: v.tensor_add(
                        evf2[pq], PS[pq][:, 0:512], evf1[pq])))(ppar),
                        waits=[("pe", e_mm), ("ldx", e_r),
                               ("stx", now("stx") - 32)],
                        inc=("ve", 1))
                    e_v = now("ve")
                    op("sy", (lambda tkk, o0, pq: (lambda s: s.dma_start(
                        out=Y[tkk * 128:(tkk + 1) * 128, o0:o0 + 512],
                        in_=evf2[pq])))(tk, oc0, ppar),
                        waits=[("ve", e_v)], inc=("stx", 16))
        E_Y = now("stx")

        GROUPS = {"sy": ("sy",), "scl": ("ac", "sc"), "ve": ("ve",),
                  "pe": ("pe",), "gps": ("gp", "po")}

        def emit_engine(eng_obj, names):
            last_wait = {}
            for (eng, waits, fn, inc) in OPS:
                if eng not in names:
                    continue
                for (s_name, val) in waits:
                    if last_wait.get(s_name, -1) >= val:
                        continue
                    eng_obj.wait_ge(S[s_name], val)
                    last_wait[s_name] = val
                ins = fn(eng_obj)
                if inc is not None:
                    ins.then_inc(S[inc[0]], inc[1])
            if "sy" in names:
                eng_obj.wait_ge(S["stx"], E_Y)

        blk.sync(lambda e: emit_engine(e, GROUPS["sy"]))
        blk.scalar(lambda e: emit_engine(e, GROUPS["scl"]))
        blk.vector(lambda e: emit_engine(e, GROUPS["ve"]))
        blk.tensor(lambda e: emit_engine(e, GROUPS["pe"]))
        blk.gpsimd(lambda e: emit_engine(e, GROUPS["gps"]))

    return nc


_NC_CACHE = {}


def kernel(L0_attn_x, L0_attn_Wq, L0_attn_Wk, L0_attn_Wv, L0_attn_Wo,
           L0_ffn_w_gate, L0_ffn_w_up, L0_ffn_w_down):
    x = np.ascontiguousarray(np.asarray(L0_attn_x, np.float32)[0])
    Wq_ = np.ascontiguousarray(np.asarray(L0_attn_Wq, np.float32))
    Wk_ = np.ascontiguousarray(np.asarray(L0_attn_Wk, np.float32))
    Wv_ = np.ascontiguousarray(np.asarray(L0_attn_Wv, np.float32))
    Wo_ = np.ascontiguousarray(np.asarray(L0_attn_Wo, np.float32))
    wg_ = np.ascontiguousarray(np.asarray(L0_ffn_w_gate, np.float32))
    wu_ = np.ascontiguousarray(np.asarray(L0_ffn_w_up, np.float32))
    wd_ = np.ascontiguousarray(np.asarray(L0_ffn_w_down, np.float32))

    if "nc" not in _NC_CACHE:
        _NC_CACHE["nc"] = build_program()
    nc = _NC_CACHE["nc"]

    in_maps = []
    for c in range(8):
        tl = tiles_of_core(c)
        rows = np.concatenate([np.arange(g * 128, (g + 1) * 128) for g in tl])
        x_own = np.ascontiguousarray(x[rows])
        cols = []
        for k in range(TPC):
            g = tl[k]
            for j in range(8 * k + 8):
                cols.append((g - j) * 128.0)
        mvec = np.asarray(cols, np.float32)
        mskm = np.broadcast_to(mvec, (128, 288)).copy()
        in_maps.append({
            "x_full": x, "x_own": x_own, "msk": mskm,
            "Wq": Wq_, "Wk": Wk_, "Wv": Wv_, "Wo": Wo_,
            "wg": wg_, "wu": wu_, "wd": wd_,
        })

    res = run_bass_kernel_spmd(nc, in_maps, core_ids=list(range(8)))

    out = np.empty((T, D), np.float32)
    for c in range(8):
        y = res.results[c]["Y"]
        tl = tiles_of_core(c)
        for k, g in enumerate(tl):
            out[g * 128:(g + 1) * 128] = y[k * 128:(k + 1) * 128]
    return out[None]
